# version 27
# baseline (speedup 1.0000x reference)
import numpy as np
from contextlib import ExitStack

import concourse.bass as bass
import concourse.mybir as mybir
from concourse.bass_utils import run_bass_kernel_spmd

F32 = mybir.dt.float32
BF16 = mybir.dt.bfloat16
FP8 = mybir.dt.float8e4
AF = mybir.ActivationFunctionType
ALU = mybir.AluOpType
AX = mybir.AxisListType

T, D, H, HD, NKV, DLAT, HFF = 4096, 1024, 8, 128, 2, 32, 4096
NT, NCH, CH = 32, 8, 512
LN_EPS = 1e-5
ATT_SCALE = float(1.0 / np.sqrt(128.0))
ABS_EPS = 1e-6

ENGINES = ('sync', 'gpsimd', 'vector', 'scalar', 'tensor')
SEMS = ('T', 'V', 'S', 'G', 'Lx0', 'Lx1', 'Lw', 'Lf', 'Ls0', 'Ls1', 'L8',
        'Lb0', 'Lb1', 'Lb2', 'Lb3', 'Lb4', 'Lb5', 'Lb6', 'Lb7',
        'Sxa', 'Sxb', 'Sf8', 'So0', 'So1')


class Plan:

    def __init__(self):
        self.ops = {e: [] for e in ENGINES}
        self.cnt = {s: 0 for s in SEMS}
        self.waited = {e: {} for e in ENGINES}

    def c(self, sem):
        return self.cnt[sem]

    def emit(self, eng, fn, deps=(), inc=None, k=1, tag=''):
        dl = []
        for (s, v) in deps:
            if v <= 0:
                continue
            assert v <= self.cnt[s], f"dep ahead of plan: {tag}: {s} {v} > {self.cnt[s]}"
            if self.waited[eng].get(s, 0) < v:
                dl.append((s, v))
                self.waited[eng][s] = v
        self.ops[eng].append((tuple(dl), fn, inc, k, tag))
        if inc is not None:
            self.cnt[inc] += k
            return self.cnt[inc]
        return None

    def flush(self, nc, semh):
        ops = {e: self.ops[e] for e in ENGINES}
        self.ops = {e: [] for e in ENGINES}
        with nc.Block() as blk:
            for eng in ENGINES:
                if not ops[eng]:
                    continue

                def body(e, _ops=ops[eng]):
                    for (dl, fn, inc, k, tag) in _ops:
                        for (s, v) in dl:
                            e.wait_ge(semh[s], v)
                        ins = fn(e)
                        if inc is not None:
                            assert ins is not None, f"{tag}: no instruction to inc"
                            ins.then_inc(semh[inc], k)
                getattr(blk, eng)(body)


def build_program(dumps=()):
    nc = bass.Bass("TRN2", target_bir_lowering=False)

    x_d = nc.dram_tensor("x", [T, D], F32, kind="ExternalInput").ap()
    wq_d = nc.dram_tensor("wq", [D, D], F32, kind="ExternalInput").ap()
    wkd_d = nc.dram_tensor("wkd", [D, NKV * DLAT], F32, kind="ExternalInput").ap()
    wvd_d = nc.dram_tensor("wvd", [D, NKV * DLAT], F32, kind="ExternalInput").ap()
    wku_d = nc.dram_tensor("wku", [DLAT, HD], F32, kind="ExternalInput").ap()
    wvu_d = nc.dram_tensor("wvu", [DLAT, HD], F32, kind="ExternalInput").ap()
    wo_d = nc.dram_tensor("wo", [D, D], F32, kind="ExternalInput").ap()
    wfc_d = nc.dram_tensor("wfc", [HFF, D], F32, kind="ExternalInput").ap()
    lnw_d = nc.dram_tensor("lnw", [HFF], F32, kind="ExternalInput").ap()
    lnb_d = nc.dram_tensor("lnb", [HFF], F32, kind="ExternalInput").ap()
    wpj_d = nc.dram_tensor("wpj", [D, HFF], F32, kind="ExternalInput").ap()
    id2_d = nc.dram_tensor("ident2", [128, 128], F32, kind="ExternalInput").ap()
    sel_d = nc.dram_tensor("selc", [2, 256], F32, kind="ExternalInput").ap()
    out_d = nc.dram_tensor("out", [T, D], F32, kind="ExternalOutput").ap()

    x1_kind = "ExternalOutput" if "x1d" in dumps else "Internal"
    x1_d = nc.dram_tensor("x1d", [T, D], F32, kind=x1_kind).ap()
    f8a_d = nc.dram_tensor("f8a", [128, NT, 8, 128], FP8, kind="Internal").ap()
    f8b_d = nc.dram_tensor("f8b", [128, 8, NT, 128], FP8, kind="Internal").ap()

    x_t = x_d.rearrange("(c t p) d -> c p t d", t=4, p=128)
    x1_t = x1_d.rearrange("(n p) d -> n p d", p=128)
    out_t = out_d.rearrange("(n p) d -> n p d", p=128)
    wq_r = wq_d.rearrange("(i p) j -> p i j", p=128)
    wo_r = wo_d.rearrange("(i p) j -> p i j", p=128)
    wkd_r = wkd_d.rearrange("(i p) j -> p i j", p=128)
    wvd_r = wvd_d.rearrange("(i p) j -> p i j", p=128)
    wfc_r = wfc_d.rearrange("(n p) d -> n p d", p=128)
    wpj_r = wpj_d.rearrange("(j p) h -> j p h", p=128)
    lnw_r = lnw_d.rearrange("(o h) -> o h", o=1)
    lnb_r = lnb_d.rearrange("(s p) -> s p", p=128)

    pl = Plan()
    dump_info = []

    with ExitStack() as st:
        en = st.enter_context
        semh = {s: en(nc.semaphore("sem_" + s)) for s in SEMS}

        def dump(name, ap, deps, cast=None):
            if name not in dumps:
                return
            dd = nc.dram_tensor("dbg_" + name, list(ap.shape), cast or ap.dtype,
                                kind="ExternalOutput").ap()
            dump_info.append(name)
            pl.emit('gpsimd', lambda e, ap=ap, dd=dd: e.dma_start(dd, ap),
                    deps=deps, inc='So0', k=16, tag='dump_' + name)

        cpool = en(nc.sbuf_tensor("cpool", [128, 6], F32))
        ident = en(nc.sbuf_tensor("ident", [128, 128], BF16))
        ones_c = en(nc.sbuf_tensor("ones_c", [128, 1], BF16))
        ones_r = en(nc.sbuf_tensor("ones_r", [1, 128], BF16))
        ln3mv = en(nc.sbuf_tensor("ln3mv", [128, NT, 2], F32))
        lnw_row = en(nc.sbuf_tensor("lnw_row", [1, HFF], BF16))
        lnb_sb = en(nc.sbuf_tensor("lnb_sb", [128, 32], BF16))
        bwq_sb = en(nc.sbuf_tensor("bwq_sb", [1, D], BF16))

        with ExitStack() as stA:
            ea = stA.enter_context
            idf32 = ea(nc.sbuf_tensor("idf32", [128, 128], F32))
            sel_sb = ea(nc.sbuf_tensor("sel_sb", [2, 256], F32))
            masks = ea(nc.sbuf_tensor("masks", [128, 4, CH], BF16))
            iotmp = ea(nc.sbuf_tensor("iotmp", [128, CH], F32))
            gacc = ea(nc.sbuf_tensor("gacc", [128, 2], F32))
            gtmp = ea(nc.sbuf_tensor("gtmp", [128, 2], F32))
            thrv = ea(nc.sbuf_tensor("thrv", [2, 4], F32))
            thr_b = ea(nc.sbuf_tensor("thr_b", [128, 4], F32))
            wq_b = ea(nc.sbuf_tensor("wq_b", [128, 8, D], BF16))
            wo_b = ea(nc.sbuf_tensor("wo_b", [128, 8, D], BF16))
            wkd_b = ea(nc.sbuf_tensor("wkd_b", [128, 8, 64], BF16))
            wvd_b = ea(nc.sbuf_tensor("wvd_b", [128, 8, 64], BF16))
            wku_b = ea(nc.sbuf_tensor("wku_b", [64, 128], BF16))
            wvu_b = ea(nc.sbuf_tensor("wvu_b", [64, 128], BF16))
            klatT = ea(nc.sbuf_tensor("klatT", [64, T], BF16))
            vlatT = ea(nc.sbuf_tensor("vlatT", [64, T], BF16))
            KT = ea(nc.sbuf_tensor("KT", [128, NKV, T], BF16))
            Vsb = ea(nc.sbuf_tensor("Vsb", [128, NKV, NT, HD], BF16))
            xc = ea(nc.sbuf_tensor("xc", [128, 2, 4, D], F32))
            x1 = ea(nc.sbuf_tensor("x1", [128, 4, D], BF16))
            x1T = ea(nc.sbuf_tensor("x1T", [128, 2, 8, CH], BF16))
            QT = ea(nc.sbuf_tensor("QT", [128, H, CH], BF16))
            ptb = ea(nc.sbuf_tensor("ptb", [128, 4, CH], BF16))
            yt = ea(nc.sbuf_tensor("yt", [128, H, CH], BF16))
            rb_sb = ea(nc.sbuf_tensor("rb_sb", [128, CH], BF16))
            rs_f = ea(nc.sbuf_tensor("rs_f", [1, CH], F32))
            rs_b = ea(nc.sbuf_tensor("rs_b", [1, 2, CH], BF16))
            asb = ea(nc.sbuf_tensor("asb", [128, 2, D], F32))
            mvln1 = ea(nc.sbuf_tensor("mvln1", [128, 4, 2], F32))
            rln1 = ea(nc.sbuf_tensor("rln1", [128, 4], F32))
            rtmp1 = ea(nc.sbuf_tensor("rtmp1", [128, 4], F32))
            mvln2 = ea(nc.sbuf_tensor("mvln2", [128, 2], F32))
            rln2 = ea(nc.sbuf_tensor("rln2", [128, 1], F32))
            rtmp2 = ea(nc.sbuf_tensor("rtmp2", [128, 1], F32))
            bn6 = ea(nc.sbuf_tensor("bn6", [128, 12], F32))
            bn6b = ea(nc.sbuf_tensor("bn6b", [128, 12], F32))
            lnb32 = ea(nc.sbuf_tensor("lnb32", [32, 128], F32))
            wstr = ea(nc.sbuf_tensor("wstr", [128, 2, D], F32))
            qa_b = ea(nc.sbuf_tensor("qa_b", [128, D], BF16))
            qn_b = ea(nc.sbuf_tensor("qn_b", [128, 2, D], BF16))
            f8ev = ea(nc.sbuf_tensor("f8ev", [128, 8, 128], FP8))
            gdump = ea(nc.sbuf_tensor("gdump", [128, D], BF16))

            ps_a = ea(nc.psum_tensor("ps_a", [128, 2, CH], F32))
            ps_b = ea(nc.psum_tensor("ps_b", [128, 2, CH], F32))
            ps_y = ea(nc.psum_tensor("ps_y", [128, CH], F32))
            ps_wo = ea(nc.psum_tensor("ps_wo", [128, 2, CH], F32))
            ps_m = ea(nc.psum_tensor("ps_m", [128, 8, 128], BF16))

            gv = 0
            for i, val in [(0, LN_EPS), (1, -0.5), (2, ATT_SCALE), (3, -10.0),
               (4, -1.0)]:
                gv = pl.emit('gpsimd', lambda e, i=i, v=val: e.memset(cpool[:, i:i + 1], v),
                             inc='G', tag=f'cp{i}')
            gv = pl.emit('gpsimd', lambda e: e.memset(ones_c[:, :], 1.0), inc='G', tag='co')
            gv = pl.emit('gpsimd', lambda e: e.memset(ones_r[:, :], 1.0), inc='G', tag='cr')
            gv = pl.emit('gpsimd', lambda e: e.memset(gacc[:, :], 0.0), inc='G', tag='cg')
            mprev = 0
            for r in range(4):
                g1 = pl.emit('gpsimd',
                             lambda e, r=r: e.iota(iotmp[:, :], [[1, CH]], base=-128 * r,
                                                   channel_multiplier=-1,
                                                   allow_small_or_imprecise_dtypes=True),
                             deps=[('V', mprev)], inc='G', tag=f'iota{r}')
                mprev = pl.emit('vector',
                        lambda e, r=r: e.tensor_scalar(masks[:, r, :], iotmp[:, :],
                                                       0.0, None, ALU.is_ge),
                        deps=[('G', g1), ('V', pl.c('V'))], inc='V', tag=f'mask{r}')
            dump('masks', masks[:, :, :], [('V', pl.c('V'))])
            ev_masks = pl.c('V')
            ev_consts = pl.c('G')

            for nm, dst, src in [('wq', wq_b[:, :, :], wq_r),
                                 ('wo', wo_b[:, :, :], wo_r),
                                 ('wkd', wkd_b[:, :, :], wkd_r),
                                 ('wvd', wvd_b[:, :, :], wvd_r),
                                 ('wku0', wku_b[0:32, :], wku_d),
                                 ('wku1', wku_b[32:64, :], wku_d),
                                 ('wvu0', wvu_b[0:32, :], wvu_d),
                                 ('wvu1', wvu_b[32:64, :], wvu_d),
                                 ('id', ident[:, :], id2_d),
                                 ('lnw', lnw_row[:, :], lnw_r)]:
                pl.emit('gpsimd', lambda e, d=dst, s=src: e.dma_start(d, s),
                        inc='Lw', k=16, tag='ld_' + nm)
            pl.emit('sync', lambda e: e.dma_start(idf32[:, :], id2_d), inc='Lf', k=16,
                    tag='ld_idf')
            pl.emit('sync', lambda e: e.dma_start(sel_sb[:, :], sel_d), inc='Lf', k=16,
                    tag='ld_sel')
            pl.emit('sync', lambda e: e.dma_start(lnb32[:, :], lnb_r), inc='Lf', k=16,
                    tag='ld_lnb')
            ev_wload = pl.c('Lw')
            ev_fload = pl.c('Lf')
            tlb = pl.emit('tensor',
                          lambda e: e.transpose(ps_a[:, 0, 0:32], lnb32[:, :],
                                                idf32[0:32, 0:32]),
                          deps=[('Lw', ev_wload), ('Lf', ev_fload)], inc='T', tag='lnbT')
            slb = pl.emit('scalar', lambda e: e.copy(lnb_sb[:, :], ps_a[:, 0, 0:32]),
                          deps=[('T', tlb)], inc='S', tag='lnbE')

            ev_xload = [None] * NCH
            ev_chunk_resid = [0] * NCH
            ev_x1trT = [0]
            ev_x1T = [0]
            ps_slot = [('S', slb), ('S', 0), ('S', 0), ('S', 0)]
            ps_m_ev = [0]
            pt_T = [0, 0, 0, 0]
            ev_psy_free = [0, 0]
            ev_b0rd = [('S', 0)]
            ev_b1rd = [('S', 0)]
            ev_woev = [('S', 0)]
            ev_wo_lastT = [0]
            ev_QTev = [0]
            ev_asb_free = [0, 0]
            ev_asb_ln3 = [0, 0]
            ev_x1_store = [0] * NT
            ev_r1b = [0]
            ev_r2b = [0]
            wstr_free = [('V', 0), ('V', 0)]
            ev_f8ev_free = [0]
            ev_qnb_free = [('T', 0), ('T', 0)]
            last_gadd = [0, 0]
            sV_prev_gadd = lambda mat: last_gadd[mat]
            ev_thr = [0]

            qunits = [(0, n) for n in range(NT)] + [(1, j * 4 + c)
                                                    for j in range(8) for c in range(4)]

            def unit_src(u):
                mat, idx = qunits[u]
                if mat == 0:
                    return wfc_r[idx]
                j, c = idx // 4, idx % 4
                return wpj_r[j][:, c * D:(c + 1) * D]

            def gen_gamma_unit(u):
                sl = u % 2
                mat = qunits[u][0]
                sx = f'Ls{sl}'
                ld = pl.emit('sync', lambda e, u=u, sl=sl: e.dma_start(wstr[:, sl, :], unit_src(u)),
                             deps=[wstr_free[sl]], inc=sx, k=16, tag=f'gld{u}')
                sR = pl.emit('vector',
                             lambda e, sl=sl, mat=mat: e.tensor_reduce(
                                 gtmp[:, mat:mat + 1], wstr[:, sl, :], AX.X, ALU.add,
                                 apply_absolute_value=True),
                             deps=[(sx, ld), ('V', pl.c('V'))], inc='V',
                             tag=f'gabs{u}')
                sV = pl.emit('vector',
                             lambda e, mat=mat: e.tensor_tensor(
                                 gacc[:, mat:mat + 1], gacc[:, mat:mat + 1],
                                 gtmp[:, mat:mat + 1], ALU.add),
                             deps=[('V', sV_prev_gadd(mat)), ('V', pl.c('V'))],
                             inc='V', tag=f'gadd{u}')
                last_gadd[mat] = sV
                wstr_free[sl] = ('V', sR)
                return
                yield

            def plan_thr():
                tT = pl.emit('tensor',
                             lambda e: e.transpose(ps_a[0:2, 0, 0:128], gacc[:, :],
                                                   idf32[:, :]),
                             deps=[('V', pl.c('V')), ps_slot[0], ('Lf', ev_fload)],
                             inc='T', tag='thrT')
                v1 = pl.emit('vector',
                             lambda e: e.tensor_reduce(thrv[0:2, 0:1], ps_a[0:2, 0, 0:128],
                                                       AX.X, ALU.add),
                             deps=[('T', tT)], inc='V', tag='thrR')
                ps_slot[0] = ('V', v1)
                v2 = pl.emit('vector',
                             lambda e: e.tensor_scalar(thrv[0:2, 2:3], thrv[0:2, 0:1],
                                                       0.5 / (HFF * D), 0.5 * ABS_EPS,
                                                       ALU.mult, ALU.add),
                             deps=[('V', v1)], inc='V', tag='thrS')
                v3 = pl.emit('vector',
                             lambda e: e.tensor_scalar(thrv[0:2, 3:4], thrv[0:2, 2:3],
                                                       -1.0, None, ALU.mult),
                             deps=[('V', v2)], inc='V', tag='thrN')
                sE = 0
                for m in range(2):
                    tB = pl.emit('tensor',
                                 lambda e, m=m: e.matmul(ps_a[:, 1, 0:2],
                                                         sel_sb[:, m * 128:(m + 1) * 128],
                                                         thrv[0:2, 2:4],
                                                         start=True, stop=True),
                                 deps=[('V', v3), ps_slot[1], ('Lf', ev_fload)],
                                 inc='T', tag=f'thrB{m}')
                    sE = pl.emit('scalar',
                                 lambda e, m=m: e.copy(thr_b[:, 2 * m:2 * m + 2],
                                                       ps_a[:, 1, 0:2]),
                                 deps=[('T', tB)], inc='S', tag=f'thrE{m}')
                    ps_slot[1] = ('S', sE)
                ev_thr[0] = sE
                dump('gacc', gacc[:, :], [('V', pl.c('V'))])
                dump('thrv', thrv[:, :], [('V', pl.c('V'))])
                dump('thrb', thr_b[:, :], [('S', sE)])

            def gen_quant_unit(u):
                mat, idx = qunits[u]
                sl = u % 2
                tc, ncc = (0, 1) if mat == 0 else (2, 3)
                sx = f'Ls{sl}'
                ld = pl.emit('sync', lambda e, u=u, sl=sl: e.dma_start(wstr[:, sl, :], unit_src(u)),
                             deps=[wstr_free[sl]], inc=sx, k=16, tag=f'qld{u}')
                v1 = pl.emit('vector',
                             lambda e, sl=sl, tc=tc: e.tensor_scalar(
                                 qa_b[:, :], wstr[:, sl, :], thr_b[:, tc:tc + 1], None,
                                 ALU.is_gt),
                             deps=[(sx, ld), ('S', ev_thr[0]), ('V', pl.c('V'))],
                             inc='V', tag=f'q1_{u}')
                yield
                qs = u % 2
                v2 = pl.emit('vector',
                             lambda e, sl=sl, ncc=ncc, qs=qs: e.scalar_tensor_tensor(
                                 qn_b[:, qs, :], wstr[:, sl, :], thr_b[:, ncc:ncc + 1],
                                 qa_b[:, :], ALU.is_lt, ALU.subtract),
                             deps=[('V', v1), ev_qnb_free[qs]], inc='V', tag=f'q2_{u}')
                wstr_free[sl] = ('V', v2)
                yield
                tl = 0
                for i in range(8):
                    tl = pl.emit('tensor',
                                 lambda e, i=i, qs=qs: e.transpose(
                                     ps_m[:, i, :],
                                     qn_b[:, qs, i * 128:(i + 1) * 128], ident[:, :]),
                                 deps=[('V', v2), ('V', ps_m_ev[0])],
                                 inc=('T' if i == 7 else None), tag=f'qT{u}_{i}')
                ev_qnb_free[qs] = ('T', tl)
                yield
                v3 = pl.emit('vector',
                             lambda e: e.tensor_scalar(f8ev[:, :, :], ps_m[:, :, :],
                                                       -1.0, None, ALU.mult),
                             deps=[('T', tl), ('Sf8', ev_f8ev_free[0])],
                             inc='V', tag=f'qE{u}')
                ps_m_ev[0] = v3
                if mat == 0:
                    dst = f8a_d[:, idx, :, :]
                else:
                    j, c = idx // 4, idx % 4
                    dst = f8b_d[:, j, c * 8:(c + 1) * 8, :]
                ev_f8ev_free[0] = pl.emit(
                    'gpsimd', lambda e, dst=dst: e.dma_start(dst, f8ev[:, :, :]),
                    deps=[('V', v3)], inc='Sf8', k=16, tag=f'qS{u}')

            ev_xload[0] = ('Lx0', pl.emit('sync',
                                  lambda e: e.dma_start(xc[:, 0, :, :], x_t[0]),
                                  inc='Lx0', k=16, tag='x0'))

            ev_x1tr = [0] * 4
            ev_x1T_ch = [0, 0]
            ev_ln1rd = [('S', 0)] * 4
            ev_rsb_rd = [0, 0]

            def gen_ln1_item(qc2, t):
                bufq = qc2 % 2
                pl.emit('vector',
                        lambda e, t=t, bufq=bufq: e.bn_stats(bn6[:, 0:6],
                                                             xc[:, bufq, t, 0:512]),
                        deps=[ev_xload[qc2], ev_ln1rd[t], ('V', pl.c('V'))],
                        inc='V', tag=f'l1a{qc2}_{t}')
                b2 = pl.emit('vector',
                             lambda e, t=t, bufq=bufq: e.bn_stats(bn6[:, 6:12],
                                                                  xc[:, bufq, t, 512:1024]),
                             deps=[], inc='V', tag=f'l1b{qc2}_{t}')
                yield
                ag = pl.emit('vector',
                             lambda e, t=t: e.bn_aggr(mvln1[:, t, :], bn6[:, :]),
                             deps=[('V', b2)], inc='V', tag=f'l1c{qc2}_{t}')
                s1 = pl.emit('scalar',
                             lambda e, t=t: e.activation(rtmp1[:, t:t + 1],
                                                         mvln1[:, t, 1:2], AF.Ln,
                                                         bias=cpool[:, 0:1]),
                             deps=[('V', ag), ('G', ev_consts), ('S', pl.c('S'))],
                             inc='S', tag=f'r1a{qc2}_{t}')
                s2 = pl.emit('scalar',
                             lambda e, t=t: e.activation(rln1[:, t:t + 1],
                                                         rtmp1[:, t:t + 1], AF.Exp,
                                                         scale=cpool[:, 1:2]),
                             deps=[('S', s1)], inc='S', tag=f'r1b{qc2}_{t}')
                ev_ln1rd[t] = ('S', s2)
                yield
                v1 = pl.emit('vector',
                             lambda e, t=t, bufq=bufq: e.tensor_scalar(
                                 x1[:, t, :], xc[:, bufq, t, :], mvln1[:, t, 0:1],
                                 rln1[:, t:t + 1], ALU.subtract, ALU.mult),
                             deps=[('S', s2), ('T', ev_x1tr[t])], inc='V',
                             tag=f'x1_{qc2}_{t}')
                yield
                tl = 0
                for i in range(8):
                    tl = pl.emit('tensor',
                                 lambda e, t=t, i=i: e.transpose(
                                     ps_m[:, i, :], x1[:, t, i * 128:(i + 1) * 128],
                                     ident[:, :]),
                                 deps=[('V', v1), ('V', ps_m_ev[0]),
                                       ('Lw', ev_wload)],
                                 inc=('T' if i == 7 else None),
                                 tag=f'x1T{qc2}_{t}_{i}')
                ev_x1tr[t] = tl
                ps_m_ev[0] = pl.emit(
                    'vector',
                    lambda e, t=t, bufq=bufq: e.tensor_copy(
                        x1T[:, bufq, :, t * 128:(t + 1) * 128], ps_m[:, :, :]),
                    deps=[('T', tl)], inc='V', tag=f'x1Te{qc2}_{t}')
                ev_x1T_ch[bufq] = ps_m_ev[0]

            for t in range(4):
                for _ in gen_ln1_item(0, t):
                    pass

            for qc in range(NCH):
                buf = qc % 2
                if qc + 1 < NCH:
                    dep = [('V', ev_chunk_resid[qc - 1])] if qc >= 1 else []
                    sx = f'Lx{(qc + 1) % 2}'
                    ev_xload[qc + 1] = (sx, pl.emit(
                        'sync', lambda e, q=qc + 1: e.dma_start(xc[:, q % 2, :, :], x_t[q]),
                        deps=dep, inc=sx, k=16, tag=f'x{qc+1}'))
                if qc == 6:
                    plan_thr()

                ev_x1T_v = ev_x1T_ch[buf]
                ev_lat = 0
                for which, wsrc, dstT in [(0, wkd_b, klatT), (1, wvd_b, vlatT)]:
                    tl = 0
                    for i in range(8):
                        r = pl.emit('tensor',
                                    lambda e, i=i, w=wsrc, which=which, buf=buf: e.matmul(
                                        ps_a[0:64, which, :], w[:, i, :],
                                        x1T[:, buf, i, :],
                                        start=(i == 0), stop=(i == 7)),
                                    deps=[('V', ev_x1T_v), ps_slot[which]],
                                    inc=('T' if i == 7 else None),
                                    tag=f'lat{which}_{qc}_{i}')
                        tl = r if r is not None else tl
                    sE = pl.emit('scalar',
                                 lambda e, d=dstT, qc=qc, which=which: e.copy(
                                     d[:, qc * CH:(qc + 1) * CH], ps_a[0:64, which, :]),
                                 deps=[('T', tl)], inc='S', tag=f'late{which}_{qc}')
                    ps_slot[which] = ('S', sE)
                    ev_lat = sE
                for g in range(NKV):
                    tK = pl.emit('tensor',
                                 lambda e, g=g, qc=qc: e.matmul(
                                     ps_b[:, 0, :], wku_b[32 * g:32 * (g + 1), :],
                                     klatT[32 * g:32 * (g + 1), qc * CH:(qc + 1) * CH],
                                     start=True, stop=True),
                                 deps=[('S', ev_lat), ps_slot[2]], inc='T',
                                 tag=f'KT{qc}_{g}')
                    sE = pl.emit('scalar',
                                 lambda e, g=g, qc=qc: e.copy(
                                     KT[:, g, qc * CH:(qc + 1) * CH], ps_b[:, 0, :]),
                                 deps=[('T', tK)], inc='S', tag=f'KTe{qc}_{g}')
                    ps_slot[2] = ('S', sE)
                    tl = 0
                    for t in range(4):
                        kt = qc * 4 + t
                        tl = pl.emit('tensor',
                                     lambda e, g=g, t=t, kt=kt: e.matmul(
                                         ps_b[:, 1, t * 128:(t + 1) * 128],
                                         vlatT[32 * g:32 * (g + 1),
                                               kt * 128:(kt + 1) * 128],
                                         wvu_b[32 * g:32 * (g + 1), :],
                                         start=True, stop=True),
                                     deps=[('S', ev_lat), ps_slot[3]], inc='T',
                                     tag=f'V{qc}_{g}_{t}')
                    sE = pl.emit('scalar',
                                 lambda e, g=g, qc=qc: e.copy(
                                     Vsb[:, g, qc * 4:(qc + 1) * 4, :]
                                     .rearrange("p t d -> p (t d)"), ps_b[:, 1, :]),
                                 deps=[('T', tl)], inc='S', tag=f'Ve{qc}_{g}')
                    ps_slot[3] = ('S', sE)
                ev_KV = pl.c('S')
                dump(f'KT{qc}', KT[:, :, :], [('S', ev_KV)])
                dump(f'V{qc}', Vsb[:, :, :, :], [('S', ev_KV)])
                for h in range(H):
                    slot = h % 4
                    pst = [ps_a[:, 0, :], ps_a[:, 1, :], ps_b[:, 0, :], ps_b[:, 1, :]][slot]
                    tl = 0
                    for i in range(8):
                        r = pl.emit('tensor',
                                    lambda e, h=h, i=i, pst=pst, buf=buf: e.matmul(
                                        pst, wq_b[:, i, h * 128:(h + 1) * 128],
                                        x1T[:, buf, i, :], start=(i == 0), stop=(i == 7)),
                                    deps=[('V', ev_x1T_v), ps_slot[slot]],
                                    inc=('T' if i == 7 else None),
                                    tag=f'QT{qc}_{h}_{i}')
                        tl = r if r is not None else tl
                    sE = pl.emit('scalar',
                                 lambda e, h=h, pst=pst: e.copy(QT[:, h, :], pst),
                                 deps=[('T', tl)], inc='S', tag=f'QTe{qc}_{h}')
                    ps_slot[slot] = ('S', sE)
                ev_QTev[0] = pl.c('S')
                dump(f'QT{qc}', QT[:, :, :], [('S', ev_QTev[0])])

                side = []
                if qc + 1 < NCH:
                    for t in range(4):
                        side.append(gen_ln1_item(qc + 1, t))
                gamma_sched = {3: range(0, 32), 4: range(32, 64)}
                quant_sched = {6: range(0, 28), 7: range(28, 64)}
                for u in gamma_sched.get(qc, []):
                    side.append(gen_gamma_unit(u))
                for u in quant_sched.get(qc, []):
                    side.append(gen_quant_unit(u))
                nstage = sum(1 for _ in ()) or None

                nk = 4 * (qc + 1)
                tasks = [(h, kt) for h in range(H) for kt in range(nk)]
                nstages = (4 * (4 if qc + 1 < NCH else 0)
                           + len(gamma_sched.get(qc, []))
                           + 3 * len(quant_sched.get(qc, [])))
                drain_every = max(1, len(tasks) // (nstages + 1))
                qk_ev = {}
                sslots = [ps_a[:, 0, :], ps_a[:, 1, :], ps_b[:, 0, :], ps_b[:, 1, :]]

                def emit_qk(i):
                    h, kt = tasks[i]
                    g = h // 4
                    sl4 = i % 4
                    qk_ev[i] = pl.emit('tensor',
                                       lambda e, g=g, h=h, kt=kt, sl4=sl4: e.matmul(
                                           sslots[sl4],
                                           KT[:, g, kt * 128:(kt + 1) * 128],
                                           QT[:, h, :], start=True, stop=True),
                                       deps=[('S', ev_QTev[0]), ps_slot[sl4],
                                             ('S', ev_KV)],
                                       inc='T', tag=f'qk{qc}_{h}_{kt}')

                pend_ep = []

                def emit_ep_start(h, tl):
                    l1 = pl.emit('scalar',
                                 lambda e: e.activation(rs_f[:, :], ps_wo[0:1, 0, :],
                                                        AF.Ln),
                                 deps=[('T', tl), ('S', pl.c('S'))], inc='S',
                                 tag=f'rc{qc}_{h}')
                    ev_b0rd[0] = ('S', l1)
                    l2 = pl.emit('scalar',
                                 lambda e, h=h: e.activation(rs_b[:, h % 2, :],
                                                             rs_f[:, :], AF.Exp,
                                                             scale=cpool[0:1, 4:5]),
                                 deps=[('S', l1), ('T', ev_rsb_rd[h % 2])], inc='S',
                                 tag=f'rcb{qc}_{h}')
                    return (h, tl, l2)

                def emit_ep_finish(h, tl, l2, cur_i):
                    bslot = (cur_i - 1) % 4
                    tb = pl.emit('tensor',
                                 lambda e, bslot=bslot, h=h: e.matmul(
                                     sslots[bslot], ones_r[:, :],
                                     rs_b[:, h % 2, :], start=True, stop=True),
                                 deps=[('S', l2), ps_slot[bslot], ev_b1rd[0]],
                                 inc='T', tag=f'bc{qc}_{h}')
                    ev_rsb_rd[h % 2] = tb
                    sE = pl.emit('vector',
                                 lambda e, bslot=bslot: e.tensor_copy(
                                     rb_sb[:, :], sslots[bslot]),
                                 deps=[('T', tb), ('V', ev_psy_free[h % 2]),
                                       ('V', pl.c('V'))],
                                 inc='V', tag=f'rbe{qc}_{h}')
                    ps_slot[bslot] = ('V', sE)
                    ev_b1rd[0] = ('V', sE)
                    psy = ps_y[:, :] if h % 2 == 0 else ps_wo[:, 1, :]
                    v3 = pl.emit('vector',
                                 lambda e, h=h, psy=psy: e.tensor_tensor(
                                     yt[:, h, :], psy, rb_sb[:, :], ALU.mult),
                                 deps=[('V', sE), ('T', tl), ('T', ev_wo_lastT[0]),
                                       ('V', pl.c('V'))],
                                 inc='V', tag=f'yt{qc}_{h}')
                    ev_psy_free[h % 2] = v3

                emit_qk(0)
                emit_qk(1)
                emit_qk(2)
                for i, (h, kt) in enumerate(tasks):
                    g = h // 4
                    sl4 = i % 4
                    tqk = qk_ev[i]
                    sE = pl.emit('scalar',
                                 lambda e, sl4=sl4: e.activation(
                                     ptb[:, sl4, :], sslots[sl4], AF.Exp,
                                     bias=cpool[:, 3:4], scale=cpool[:, 2:3]),
                                 deps=[('T', tqk), ('T', pt_T[sl4]),
                                       ('G', ev_consts)],
                                 inc='S', tag=f'exp{qc}_{h}_{kt}')
                    ps_slot[sl4] = ('S', sE)
                    vm = 0
                    r = kt - 4 * qc
                    if r >= 0:
                        vm = pl.emit('vector',
                                     lambda e, sl4=sl4, r=r: e.tensor_tensor(
                                         ptb[:, sl4, :], ptb[:, sl4, :],
                                         masks[:, r, :], ALU.mult),
                                     deps=[('S', sE), ('V', ev_masks),
                                           ('V', pl.c('V'))],
                                     inc='V', tag=f'msk{qc}_{h}_{kt}')
                    if pend_ep and i >= pend_ep[0][0]:
                        emit_ep_finish(*pend_ep.pop(0)[1], i)
                    if i + 3 < len(tasks):
                        emit_qk(i + 3)
                    pdep = [('V', vm)] if vm else [('S', sE)]
                    pl.emit('tensor',
                            lambda e, sl4=sl4, stt=(kt == 0): e.matmul(
                                ps_wo[0:1, 0, :], ones_c[:, :],
                                ptb[:, sl4, :], start=stt, stop=True,
                                skip_group_check=True),
                            deps=pdep + [ev_b0rd[0], ev_woev[0]],
                            tag=f'ss{qc}_{h}_{kt}')
                    psy = ps_y[:, :] if h % 2 == 0 else ps_wo[:, 1, :]
                    tl = pl.emit('tensor',
                                 lambda e, g=g, kt=kt, sl4=sl4, stt=(kt == 0),
                                        psy=psy: e.matmul(
                                     psy, Vsb[:, g, kt, :],
                                     ptb[:, sl4, :], start=stt, stop=True,
                                     skip_group_check=True),
                                 deps=[('V', ev_psy_free[h % 2])] + pdep,
                                 inc='T', tag=f'pv{qc}_{h}_{kt}')
                    pt_T[sl4] = tl
                    if kt == nk - 1:
                        pend_ep.append((i + 5, emit_ep_start(h, tl)))
                    if side and (i + 1) % drain_every == 0:
                        try:
                            next(side[0])
                        except StopIteration:
                            side.pop(0)
                while side:
                    try:
                        next(side[0])
                    except StopIteration:
                        side.pop(0)
                while pend_ep:
                    emit_ep_finish(*pend_ep.pop(0)[1], len(tasks) + 4)
                ev_yt_all = pl.c('V')
                dump(f'yt{qc}', yt[:, :, :], [('V', ev_yt_all)])
                for t in range(4):
                    tile = qc * 4 + t
                    slot = t % 2
                    pool = [ps_wo, ps_a, ps_b][t % 3]
                    if t % 3 == 0:
                        pdeps = [ev_woev[0], ev_b0rd[0], ('V', ev_psy_free[1])]
                    elif t % 3 == 1:
                        pdeps = [ps_slot[0], ps_slot[1]]
                    else:
                        pdeps = [ps_slot[2], ps_slot[3]]
                    tl = 0
                    for h in range(H):
                        for j in range(2):
                            r = pl.emit('tensor',
                                        lambda e, h=h, j=j, t=t, pool=pool: e.matmul(
                                            pool[:, j, :],
                                            yt[:, h, t * 128:(t + 1) * 128],
                                            wo_b[:, h, j * 512:(j + 1) * 512],
                                            start=(h == 0), stop=(h == H - 1)),
                                        deps=[('V', ev_yt_all)] + pdeps,
                                        inc=('T' if h == H - 1 else None),
                                        tag=f'wo{qc}_{t}_{h}_{j}')
                            tl = r if r is not None else tl
                    ev_wo_lastT[0] = tl
                    sE = pl.emit('vector',
                                 lambda e, slot=slot, pool=pool: e.tensor_copy(
                                     asb[:, slot, :],
                                     pool[:, :, :].rearrange("p j c -> p (j c)")),
                                 deps=[('T', tl),
                                       ('Sxa' if slot == 0 else 'Sxb', ev_asb_free[slot]),
                                       ('V', ev_asb_ln3[slot]), ('V', pl.c('V'))],
                                 inc='V', tag=f'woe{qc}_{t}')
                    if t % 3 == 0:
                        ev_woev[0] = ('V', sE)
                        ev_psy_free[1] = sE
                    elif t % 3 == 1:
                        ps_slot[0] = ('V', sE)
                        ps_slot[1] = ('V', sE)
                    else:
                        ps_slot[2] = ('V', sE)
                        ps_slot[3] = ('V', sE)
                    pl.emit('vector',
                            lambda e, slot=slot: e.bn_stats(bn6b[:, 0:6],
                                                            asb[:, slot, 0:512]),
                            deps=[('V', sE), ('S', ev_r2b[0]), ('V', pl.c('V'))],
                            inc='V', tag=f'l2a{qc}_{t}')
                    b2 = pl.emit('vector',
                                 lambda e, slot=slot: e.bn_stats(bn6b[:, 6:12],
                                                                 asb[:, slot, 512:1024]),
                                 deps=[], inc='V', tag=f'l2b{qc}_{t}')
                    b3 = pl.emit('vector',
                                 lambda e: e.bn_aggr(mvln2[:, :], bn6b[:, :]),
                                 deps=[('V', b2)], inc='V', tag=f'l2c{qc}_{t}')
                    s1 = pl.emit('scalar',
                                 lambda e: e.activation(rtmp2[:, :], mvln2[:, 1:2], AF.Ln,
                                                        bias=cpool[:, 0:1]),
                                 deps=[('V', b3)], inc='S', tag=f'l2d{qc}_{t}')
                    s2 = pl.emit('scalar',
                                 lambda e: e.activation(rln2[:, :], rtmp2[:, :], AF.Exp,
                                                        scale=cpool[:, 1:2]),
                                 deps=[('S', s1)], inc='S', tag=f'l2e{qc}_{t}')
                    ev_r2b[0] = s2
                    v1 = pl.emit('vector',
                                 lambda e, slot=slot: e.tensor_scalar(
                                     asb[:, slot, :], asb[:, slot, :], mvln2[:, 0:1],
                                     rln2[:, 0:1], ALU.subtract, ALU.mult),
                                 deps=[('S', s2), ('V', pl.c('V'))], inc='V',
                                 tag=f'l2f{qc}_{t}')
                    v2 = pl.emit('vector',
                                 lambda e, slot=slot, t=t, buf=buf: e.tensor_tensor(
                                     asb[:, slot, :], asb[:, slot, :], xc[:, buf, t, :],
                                     ALU.add),
                                 deps=[('V', v1)], inc='V', tag=f'res{qc}_{t}')
                    if t == 3:
                        ev_chunk_resid[qc] = v2
                    pl.emit('vector',
                            lambda e, slot=slot: e.bn_stats(bn6b[:, 0:6],
                                                            asb[:, slot, 0:512]),
                            deps=[('V', v2)], inc='V', tag=f'l3a{qc}_{t}')
                    b2 = pl.emit('vector',
                                 lambda e, slot=slot: e.bn_stats(bn6b[:, 6:12],
                                                                 asb[:, slot, 512:1024]),
                                 deps=[], inc='V', tag=f'l3b{qc}_{t}')
                    b3 = pl.emit('vector',
                                 lambda e, tile=tile: e.bn_aggr(ln3mv[:, tile, :],
                                                                bn6b[:, :]),
                                 deps=[('V', b2)], inc='V', tag=f'l3c{qc}_{t}')
                    ev_asb_ln3[slot] = b3
                    ev_x1_store[tile] = pl.emit(
                        'gpsimd',
                        lambda e, slot=slot, tile=tile: e.dma_start(x1_t[tile],
                                                                    asb[:, slot, :]),
                        deps=[('V', v2)], inc=('Sxa' if slot == 0 else 'Sxb'),
                        k=16, tag=f'x1s{qc}_{t}')
                    ev_asb_free[slot] = ev_x1_store[tile]


            endA = {}
            endA['T'] = pl.emit('tensor', lambda e: e.nop(), inc='T', tag='fT')
            endA['V'] = pl.emit('vector', lambda e: e.nop(), inc='V', tag='fV')
            endA['S'] = pl.emit('scalar', lambda e: e.nop(), inc='S', tag='fS')
            endA['G'] = pl.emit('gpsimd', lambda e: e.nop(), inc='G', tag='fG')
            endA['Sxa'] = pl.c('Sxa')
            endA['Sxb'] = pl.c('Sxb')
            endA['Sf8'] = pl.c('Sf8')
            fence = [('T', endA['T']), ('V', endA['V']), ('S', endA['S']),
                     ('G', endA['G'])]
            pl.flush(nc, semh)

        with ExitStack() as stB:
            eb = stB.enter_context
            wfc8 = eb(nc.sbuf_tensor("wfc8", [128, NT, 8, 128], FP8))
            wp8 = eb(nc.sbuf_tensor("wp8", [128, 8, NT, 128], FP8))
            w_rep = eb(nc.sbuf_tensor("w_rep", [128, HFF], BF16))
            x1t = eb(nc.sbuf_tensor("x1t", [128, 8, D], F32))
            h_g = eb(nc.sbuf_tensor("h_g", [128, 2, HFF], F32))
            zb = eb(nc.sbuf_tensor("zb", [128, 2, HFF], BF16))
            x2 = eb(nc.sbuf_tensor("x2", [128, 2, D], BF16))
            x2T = eb(nc.sbuf_tensor("x2T", [128, 2, 8, 128], BF16))
            gsum = eb(nc.sbuf_tensor("gsum", [128, 2, 8], F32))
            qsum = eb(nc.sbuf_tensor("qsum", [128, 2, 8], F32))
            sqd = eb(nc.sbuf_tensor("sqd", [128, 512], BF16))
            stt1 = eb(nc.sbuf_tensor("stt1", [128, 4], F32))
            zT = eb(nc.sbuf_tensor("zT", [128, NT, 128], BF16))
            o2sb = eb(nc.sbuf_tensor("o2sb", [128, 2, D], F32))
            bwq_rep = eb(nc.sbuf_tensor("bwq_rep", [128, D], F32))
            outb = eb(nc.sbuf_tensor("outb", [128, 2, D], F32))
            r3all = eb(nc.sbuf_tensor("r3all", [128, NT], F32))
            r3tmp = eb(nc.sbuf_tensor("r3tmp", [128, NT], F32))
            hmv = eb(nc.sbuf_tensor("hmv", [128, 4, 2], F32))
            rhh = eb(nc.sbuf_tensor("rhh", [128, 2], F32))
            rht = eb(nc.sbuf_tensor("rht", [128, 2], F32))
            o4mv = eb(nc.sbuf_tensor("o4mv", [128, 2, 2], F32))
            r44 = eb(nc.sbuf_tensor("r44", [128, 2], F32))
            r4t = eb(nc.sbuf_tensor("r4t", [128, 2], F32))
            bnB = eb(nc.sbuf_tensor("bnB", [128, 12], F32))

            ps_h = eb(nc.psum_tensor("ps_h", [128, 2, 512], F32))
            ps_ztp = eb(nc.psum_tensor("ps_ztp", [128, 2, 16, 128], BF16))
            ps_o2 = eb(nc.psum_tensor("ps_o2", [128, 2, 512], F32))

            pl.emit('sync', lambda e: e.dma_start(wfc8[:, :, :, :], f8a_d),
                    deps=[('Sf8', endA['Sf8']), ('Sxa', endA['Sxa']),
                          ('Sxb', endA['Sxb'])] + fence,
                    inc='L8', k=16, tag='ldf8a')
            ldB = pl.emit('sync', lambda e: e.dma_start(wp8[:, :, :, :], f8b_d),
                          deps=[], inc='L8', k=16, tag='ldf8b')
            ev_w8 = ('L8', ldB)
            ldA = ldB

            ev_wr = 0
            for c in range(8):
                tb = pl.emit('tensor',
                             lambda e, c=c: e.matmul(ps_o2[:, 0, :], ones_r[:, :],
                                                     lnw_row[0:1, c * 512:(c + 1) * 512],
                                                     start=True, stop=True),
                             deps=fence + [('V', ev_wr), ('Lw', ev_wload)],
                             inc='T', tag=f'wr{c}')
                ev_wr = pl.emit('vector',
                                lambda e, c=c: e.tensor_copy(
                                    w_rep[:, c * 512:(c + 1) * 512], ps_o2[:, 0, :]),
                                deps=[('T', tb)], inc='V', tag=f'wre{c}')

            s1 = pl.emit('scalar',
                         lambda e: e.activation(r3tmp[:, :], ln3mv[:, :, 1], AF.Ln,
                                                bias=cpool[:, 0:1]),
                         deps=fence, inc='S', tag='r3a')
            ev_r3 = pl.emit('scalar',
                            lambda e: e.activation(r3all[:, :], r3tmp[:, :], AF.Exp,
                                                   scale=cpool[:, 1:2]),
                            deps=[('S', s1)], inc='S', tag='r3b')

            tl = 0
            for dc in range(2):
                for s in range(NT):
                    r = pl.emit('tensor',
                                lambda e, dc=dc, s=s: e.matmul(
                                    ps_h[0:1, dc, :], lnb_sb[:, s:s + 1],
                                    wp8[:, 4 * dc:4 * dc + 4, s, :],
                                    start=(s == 0), stop=(s == NT - 1)),
                                deps=[ev_w8, ('Lw', ev_wload)],
                                inc=('T' if s == NT - 1 else None),
                                tag=f'bwq{dc}_{s}')
                    tl = r if r is not None else tl
            ev_bwq = pl.emit('scalar',
                             lambda e: e.copy(bwq_sb[:, :],
                                              ps_h[0:1, :, :].rearrange("p j c -> p (j c)")),
                             deps=[('T', tl)], inc='S', tag='bwqe')
            ev_brep = 0
            for dc in range(2):
                tb = pl.emit('tensor',
                             lambda e, dc=dc: e.matmul(ps_o2[:, dc, :], ones_r[:, :],
                                                       bwq_sb[0:1, dc * 512:(dc + 1) * 512],
                                                       start=True, stop=True),
                             deps=[('S', ev_bwq), ('V', ev_wr)], inc='T',
                             tag=f'brep{dc}')
                ev_brep = pl.emit('vector',
                                  lambda e, dc=dc: e.tensor_copy(
                                      bwq_rep[:, dc * 512:(dc + 1) * 512],
                                      ps_o2[:, dc, :]),
                                  deps=[('T', tb)], inc='V', tag=f'brepe{dc}')
            ev_o2_rd0 = ev_brep

            dump('wfc8', wfc8[:, :, :, :], [('L8', ldA)], cast=F32)
            dump('wp8', wp8[:, :, :, :], [('L8', ldB)], cast=F32)
            dump('bwq', bwq_sb[:, :], [('S', ev_bwq)], cast=F32)
            dump('wrep', w_rep[:, :], [('V', ev_wr)], cast=F32)
            ev_x1load = [None] * NT
            ev_x1slot_free = [0] * 8
            ev_hslot_free = [0, 0]
            ev_x2T_ev = [0]
            ev_fc_last = [0]
            ev_zT_free = [0]
            ev_o2sb_free = [0, 0]
            ev_o2e = [ev_wr, ev_wr]
            ev_o2_rd = [None]
            ev_outb_free = [0, 0]
            ev_ztp_ev = [0]
            ev_gelu = [0, 0]
            ev_hmv = [0, 0, 0, 0]
            ev_o4mv = [0, 0]
            ev_rh = [0]
            ev_r4 = [0]
            pend = []

            for t in range(4):
                sx = f'Lb{t % 8}'
                ev_x1load[t] = (sx, pl.emit('sync',
                                       lambda e, t=t: e.dma_start(x1t[:, t, :], x1_t[t]),
                                       deps=[('Sxa', endA['Sxa']),
                                             ('Sxb', endA['Sxb'])] + fence,
                                       inc=sx, k=16, tag=f'bx{t}'))

            def ln4_apply(tile):
                par = tile % 2
                slot = tile % 8
                v1 = pl.emit('vector',
                             lambda e, par=par: e.tensor_scalar(
                                 o2sb[:, par, :], o2sb[:, par, :], o4mv[:, par, 0:1],
                                 r44[:, par:par + 1], ALU.subtract, ALU.mult),
                             deps=[('S', ev_r4[0]), ('V', ev_o2e[par]),
                                   ('V', pl.c('V'))],
                             inc='V', tag=f'l4f{tile}')
                v2 = pl.emit('vector',
                             lambda e, par=par, slot=slot: e.tensor_tensor(
                                 outb[:, par, :], o2sb[:, par, :], x1t[:, slot, :],
                                 ALU.add),
                             deps=[('V', v1), (f'So{par}', ev_outb_free[par])],
                             inc='V', tag=f'l4r{tile}')
                if tile == 0:
                    dump('ap_o4mv', o4mv[:, :, :], [('S', ev_r4[0])])
                    dump('ap_r44', r44[:, :], [('S', ev_r4[0])])
                    dump('ap_o2sb', o2sb[:, 0, :], [('V', v1)])
                    dump('ap_x1t', x1t[:, 0, :], [('V', v2)])
                    dump('ap_outb', outb[:, 0, :], [('V', v2)])
                ev_x1slot_free[slot] = v2
                ev_o2sb_free[par] = v2
                ev_outb_free[par] = pl.emit(
                    'gpsimd',
                    lambda e, par=par, tile=tile: e.dma_start(out_t[tile],
                                                              outb[:, par, :]),
                    deps=[('V', v2)], inc=f'So{par}', k=16, tag=f'outs{tile}')

            ev_ztp = [0, 0]
            ev_o2_rd[0] = ev_o2_rd0
            ev_r44_rd = [0]

            ev_x2st = [('V', 0), ('V', 0)]

            def plan_x2stage(tile):
                xb = tile % 2
                v1 = pl.emit('vector',
                             lambda e, tile=tile, slot=tile % 8, xb=xb: e.tensor_scalar(
                                 x2[:, xb, :], x1t[:, slot, :], ln3mv[:, tile, 0:1],
                                 r3all[:, tile:tile + 1], ALU.subtract, ALU.mult),
                             deps=[ev_x1load[tile], ('S', ev_r3),
                                   ('T', ev_fc_last[0]), ('V', pl.c('V'))],
                             inc='V', tag=f'x2_{tile}')
                tl = 0
                for i in range(8):
                    tl = pl.emit('tensor',
                                 lambda e, i=i, xb=xb: e.transpose(
                                     ps_ztp[:, 0, i, :],
                                     x2[:, xb, i * 128:(i + 1) * 128], ident[:, :]),
                                 deps=[('V', v1), ('V', ev_ztp[0])],
                                 inc=('T' if i == 7 else None), tag=f'x2T{tile}_{i}')
                vE = pl.emit('vector',
                             lambda e, xb=xb: e.tensor_copy(x2T[:, xb, :, :],
                                                            ps_ztp[:, 0, 0:8, :]),
                             deps=[('T', tl)], inc='V', tag=f'x2Te{tile}')
                ev_ztp[0] = vE
                ev_x2st[xb] = ('V', vE)

            def plan_fc(tile):
                par = tile % 2
                xb = tile % 2
                for hc in range(8):
                    pb = hc % 2
                    tl = 0
                    for i in range(8):
                        r = pl.emit('tensor',
                                    lambda e, hc=hc, i=i, pb=pb, xb=xb: e.matmul(
                                        ps_h[:, pb, :], x2T[:, xb, i, :],
                                        wfc8[:, 4 * hc:4 * hc + 4, i, :],
                                        start=(i == 0), stop=(i == 7)),
                                    deps=[ev_x2st[xb], ev_w8,
                                          ('S', ev_gelu[pb]), ('S', ev_bwq)],
                                    inc=('T' if i == 7 else None),
                                    tag=f'fc{tile}_{hc}_{i}')
                        tl = r if r is not None else tl
                    ev_gelu[pb] = pl.emit(
                        'scalar',
                        lambda e, hc=hc, pb=pb, par=par: e.activation(
                            h_g[:, par, hc * 512:(hc + 1) * 512], ps_h[:, pb, :],
                            AF.Gelu, accum_out=gsum[:, par, hc:hc + 1]),
                        deps=[('T', tl), ('V', ev_hslot_free[par])],
                        inc='S', tag=f'gel{tile}_{hc}')
                    pl.emit('scalar',
                            lambda e, hc=hc, par=par: e.activation(
                                sqd[:, :], h_g[:, par, hc * 512:(hc + 1) * 512],
                                AF.Square, accum_out=qsum[:, par, hc:hc + 1]),
                            deps=[('S', ev_gelu[pb])], inc='S', tag=f'sq{tile}_{hc}')
                ev_fc_last[0] = tl
                ev_sq = pl.c('S')
                v1 = pl.emit('vector',
                             lambda e, par=par: e.tensor_reduce(
                                 stt1[:, 0:1], gsum[:, par, :], AX.X, ALU.add),
                             deps=[('S', ev_sq), ('V', pl.c('V'))], inc='V',
                             tag=f'st1{tile}')
                v2 = pl.emit('vector',
                             lambda e, par=par: e.tensor_reduce(
                                 stt1[:, 1:2], qsum[:, par, :], AX.X, ALU.add),
                             deps=[], inc='V', tag=f'st2{tile}')
                h4 = tile % 4
                v3 = pl.emit('vector',
                             lambda e, h4=h4: e.tensor_scalar(
                                 hmv[:, h4, 0:1], stt1[:, 0:1], 1.0 / HFF, None,
                                 ALU.mult),
                             deps=[('V', v2), ('S', ev_rh[0])], inc='V',
                             tag=f'st3{tile}')
                v4 = pl.emit('vector',
                             lambda e, h4=h4: e.tensor_tensor(
                                 stt1[:, 2:3], hmv[:, h4, 0:1],
                                 hmv[:, h4, 0:1], ALU.mult),
                             deps=[('V', v3)], inc='V', tag=f'st4{tile}')
                ev_hmv[h4] = pl.emit('vector',
                                     lambda e, h4=h4: e.scalar_tensor_tensor(
                                         hmv[:, h4, 1:2], stt1[:, 1:2], 1.0 / HFF,
                                         stt1[:, 2:3], ALU.mult, ALU.subtract),
                                     deps=[('V', v4)], inc='V', tag=f'st5{tile}')
                if tile <= 1:
                    dump(f'hmv{tile}', hmv[:, :, :], [('V', ev_hmv[h4])])
                    dump(f'hg{tile}', h_g[:, tile % 2, :], [('V', ev_hmv[h4])])

            def plan_rstd_h(g):
                c0 = (2 * g) % 4
                s1 = pl.emit('scalar',
                             lambda e, c0=c0: e.activation(rht[:, :],
                                                           hmv[:, c0:c0 + 2, 1],
                                                           AF.Ln, bias=cpool[:, 0:1]),
                             deps=[('V', ev_hmv[c0]), ('V', ev_hmv[c0 + 1]),
                                   ('V', pl.c('V'))],
                             inc='S', tag=f'rha{g}')
                ev_rh[0] = pl.emit('scalar',
                                   lambda e: e.activation(rhh[:, :], rht[:, :], AF.Exp,
                                                          scale=cpool[:, 1:2]),
                                   deps=[('S', s1)], inc='S', tag=f'rhb{g}')

            def plan_r4(g):
                s3 = pl.emit('scalar',
                             lambda e: e.activation(r4t[:, :], o4mv[:, :, 1], AF.Ln,
                                                    bias=cpool[:, 0:1]),
                             deps=[('V', ev_o4mv[0]), ('V', ev_o4mv[1]),
                                   ('V', ev_r44_rd[0])],
                             inc='S', tag=f'r4a{g}')
                ev_r4[0] = pl.emit('scalar',
                                   lambda e: e.activation(r44[:, :], r4t[:, :], AF.Exp,
                                                          scale=cpool[:, 1:2]),
                                   deps=[('S', s3)], inc='S', tag=f'r4b{g}')

            def plan_applies():
                for pt_ in pend:
                    ln4_apply(pt_)
                if pend:
                    ev_r44_rd[0] = pl.c('V')
                pend.clear()

            ev_zbfree = [('T', 0), ('T', 0)]

            def plan_z(tt):
                pz = tt % 2
                v1 = pl.emit('vector',
                             lambda e, pz=pz, h4=tt % 4: e.tensor_scalar(
                                 h_g[:, pz, :], h_g[:, pz, :], hmv[:, h4, 0:1],
                                 None, ALU.subtract),
                             deps=[('V', ev_hmv[tt % 4]), ('V', pl.c('V'))],
                             inc='V', tag=f'z1_{tt}')
                v2 = pl.emit('vector',
                             lambda e, pz=pz: e.tensor_tensor(
                                 zb[:, pz, :], h_g[:, pz, :], w_rep[:, :], ALU.mult),
                             deps=[('V', v1), ev_zbfree[pz], ('V', ev_wr)],
                             inc='V', tag=f'z2_{tt}')
                ev_hslot_free[pz] = v2
                return v2

            def plan_ztproj(tt, v2):
                pz = tt % 2
                for w in range(2):
                    tl = 0
                    for i in range(16):
                        s_ = w * 16 + i
                        tl = pl.emit('tensor',
                                     lambda e, s_=s_, i=i, w=w, pz=pz: e.transpose(
                                         ps_ztp[:, w, i, :],
                                         zb[:, pz, s_ * 128:(s_ + 1) * 128],
                                         ident[:, :]),
                                     deps=[('V', v2), ('V', ev_ztp[w])],
                                     inc=('T' if i == 15 else None),
                                     tag=f'zT{tt}_{s_}')
                    ev_ztp[w] = pl.emit(
                        'vector',
                        lambda e, w=w: e.tensor_copy(
                            zT[:, w * 16:(w + 1) * 16, :], ps_ztp[:, w, :, :]),
                        deps=[('T', tl)], inc='V', tag=f'zTe{tt}_{w}')
                ev_zbfree[pz] = ('T', tl)
                tl = 0
                for dc in range(2):
                    for s_ in range(NT):
                        r = pl.emit('tensor',
                                    lambda e, dc=dc, s_=s_: e.matmul(
                                        ps_o2[:, dc, :], zT[:, s_, :],
                                        wp8[:, 4 * dc:4 * dc + 4, s_, :],
                                        start=(s_ == 0), stop=(s_ == NT - 1)),
                                    deps=[('V', ev_ztp[0]), ('V', ev_ztp[1]),
                                          ('V', ev_o2_rd[0])],
                                    inc=('T' if s_ == NT - 1 else None),
                                    tag=f'pj{tt}_{dc}_{s_}')
                        tl = r if r is not None else tl
                ev_zT_free[0] = tl
                v4 = pl.emit('vector',
                             lambda e, pz=pz: e.scalar_tensor_tensor(
                                 o2sb[:, pz, :],
                                 ps_o2[:, :, :].rearrange("p j c -> p (j c)"),
                                 rhh[:, pz:pz + 1], bwq_rep[:, :],
                                 ALU.mult, ALU.add),
                             deps=[('T', tl), ('V', ev_o2sb_free[pz]),
                                   ('S', ev_rh[0]), ('V', ev_brep)],
                             inc='V', tag=f'o2e{tt}')
                ev_o2e[pz] = v4
                ev_o2_rd[0] = v4
                if tt == 0:
                    dump('o2sb0', o2sb[:, 0, :], [('V', v4)])
                    dump('rhh0', rhh[:, :], [('V', v4)])
                b1 = pl.emit('vector',
                             lambda e, pz=pz: e.bn_stats(bnB[:, 0:6], o2sb[:, pz, 0:512]),
                             deps=[('V', v4)], inc='V', tag=f'l4a{tt}')
                b2 = pl.emit('vector',
                             lambda e, pz=pz: e.bn_stats(bnB[:, 6:12],
                                                         o2sb[:, pz, 512:1024]),
                             deps=[], inc='V', tag=f'l4b{tt}')
                b3 = pl.emit('vector',
                             lambda e, pz=pz: e.bn_aggr(o4mv[:, pz, :], bnB[:, :]),
                             deps=[('V', b2)], inc='V', tag=f'l4c{tt}')
                ev_o4mv[pz] = b3
                pend.append(tt)

            plan_x2stage(0)
            for tile in range(NT):
                if tile + 1 < NT:
                    plan_x2stage(tile + 1)
                zpair = None
                if tile >= 2 and tile % 2 == 0:
                    plan_applies()
                    zpair = (plan_z(tile - 2), plan_z(tile - 1))
                plan_fc(tile)
                if zpair is not None:
                    plan_ztproj(tile - 2, zpair[0])
                    plan_ztproj(tile - 1, zpair[1])
                if tile % 2 == 1:
                    plan_rstd_h((tile - 1) // 2)
                    if tile >= 3:
                        plan_r4((tile - 3) // 2)
                if tile + 4 < NT:
                    nt_ = tile + 4
                    sx = f'Lb{nt_ % 8}'
                    ev_x1load[nt_] = (sx, pl.emit(
                        'sync',
                        lambda e, nt_=nt_: e.dma_start(x1t[:, nt_ % 8, :], x1_t[nt_]),
                        deps=[('Sxa', endA['Sxa']), ('Sxb', endA['Sxb']),
                              ('V', ev_x1slot_free[nt_ % 8])],
                        inc=sx, k=16, tag=f'bx{nt_}'))

            plan_applies()
            za = plan_z(NT - 2)
            zbv = plan_z(NT - 1)
            plan_ztproj(NT - 2, za)
            plan_ztproj(NT - 1, zbv)
            plan_r4(NT // 2 - 1)
            plan_applies()

            pl.emit('gpsimd', lambda e: None,
                    deps=[('So0', pl.c('So0')), ('So1', pl.c('So1'))], tag='final')
            pl.flush(nc, semh)

    return nc, dump_info


def kernel(L0_attn_x, L0_attn_wq, L0_attn_wk_down, L0_attn_wv_down,
           L0_attn_wk_up, L0_attn_wv_up, L0_attn_wo,
           L0_ffn_W_fc, L0_ffn_ln_w, L0_ffn_ln_b, L0_ffn_W_proj, _dumps=()):
    f = lambda a: np.ascontiguousarray(np.asarray(a, dtype=np.float32))
    x = f(L0_attn_x)
    sel = np.zeros((2, 256), dtype=np.float32)
    sel[0, 0:128] = 1.0
    sel[1, 128:256] = 1.0
    ws = {
        "wq": f(L0_attn_wq), "wkd": f(L0_attn_wk_down), "wvd": f(L0_attn_wv_down),
        "wku": f(L0_attn_wk_up), "wvu": f(L0_attn_wv_up), "wo": f(L0_attn_wo),
        "wfc": f(L0_ffn_W_fc), "lnw": f(L0_ffn_ln_w), "lnb": f(L0_ffn_ln_b),
        "wpj": f(L0_ffn_W_proj),
        "ident2": np.eye(128, dtype=np.float32),
        "selc": sel,
    }
    nc, _ = build_program(dumps=_dumps)
    in_maps = [dict(ws, x=x[i]) for i in range(8)]
    res = run_bass_kernel_spmd(nc, in_maps, core_ids=list(range(8)))
    out = np.stack([res.results[i]["out"] for i in range(8)], axis=0)
    if _dumps:
        kernel._last_results = res
    return out.reshape(8, T, D)


# version 29
# speedup vs baseline: 1.0541x; 1.0541x over previous
import numpy as np
from contextlib import ExitStack

import concourse.bass as bass
import concourse.mybir as mybir
from concourse.bass_utils import run_bass_kernel_spmd

F32 = mybir.dt.float32
BF16 = mybir.dt.bfloat16
FP8 = mybir.dt.float8e4
AF = mybir.ActivationFunctionType
ALU = mybir.AluOpType
AX = mybir.AxisListType

T, D, H, HD, NKV, DLAT, HFF = 4096, 1024, 8, 128, 2, 32, 4096
NT, NCH, CH = 32, 8, 512
LN_EPS = 1e-5
ATT_SCALE = float(1.0 / np.sqrt(128.0))
ABS_EPS = 1e-6

ENGINES = ('sync', 'gpsimd', 'vector', 'scalar', 'tensor')
SEMS = ('T', 'V', 'S', 'G', 'Lx0', 'Lx1', 'Lw', 'Lf', 'Ls0', 'Ls1', 'L8',
        'Lb0', 'Lb1', 'Lb2', 'Lb3', 'Lb4', 'Lb5', 'Lb6', 'Lb7',
        'Sxa', 'Sxb', 'Sf8', 'So0', 'So1')


class Plan:

    def __init__(self):
        self.ops = {e: [] for e in ENGINES}
        self.cnt = {s: 0 for s in SEMS}
        self.waited = {e: {} for e in ENGINES}

    def c(self, sem):
        return self.cnt[sem]

    def emit(self, eng, fn, deps=(), inc=None, k=1, tag=''):
        dl = []
        for (s, v) in deps:
            if v <= 0:
                continue
            assert v <= self.cnt[s], f"dep ahead of plan: {tag}: {s} {v} > {self.cnt[s]}"
            if self.waited[eng].get(s, 0) < v:
                dl.append((s, v))
                self.waited[eng][s] = v
        self.ops[eng].append((tuple(dl), fn, inc, k, tag))
        if inc is not None:
            self.cnt[inc] += k
            return self.cnt[inc]
        return None

    def flush(self, nc, semh):
        ops = {e: self.ops[e] for e in ENGINES}
        self.ops = {e: [] for e in ENGINES}
        with nc.Block() as blk:
            for eng in ENGINES:
                if not ops[eng]:
                    continue

                def body(e, _ops=ops[eng]):
                    for (dl, fn, inc, k, tag) in _ops:
                        for (s, v) in dl:
                            e.wait_ge(semh[s], v)
                        ins = fn(e)
                        if inc is not None:
                            assert ins is not None, f"{tag}: no instruction to inc"
                            ins.then_inc(semh[inc], k)
                getattr(blk, eng)(body)


def build_program(dumps=()):
    nc = bass.Bass("TRN2", target_bir_lowering=False)

    x_d = nc.dram_tensor("x", [T, D], F32, kind="ExternalInput").ap()
    wq_d = nc.dram_tensor("wq", [D, D], F32, kind="ExternalInput").ap()
    wkd_d = nc.dram_tensor("wkd", [D, NKV * DLAT], F32, kind="ExternalInput").ap()
    wvd_d = nc.dram_tensor("wvd", [D, NKV * DLAT], F32, kind="ExternalInput").ap()
    wku_d = nc.dram_tensor("wku", [DLAT, HD], F32, kind="ExternalInput").ap()
    wvu_d = nc.dram_tensor("wvu", [DLAT, HD], F32, kind="ExternalInput").ap()
    wo_d = nc.dram_tensor("wo", [D, D], F32, kind="ExternalInput").ap()
    wfc_d = nc.dram_tensor("wfc", [HFF, D], F32, kind="ExternalInput").ap()
    lnw_d = nc.dram_tensor("lnw", [HFF], F32, kind="ExternalInput").ap()
    lnb_d = nc.dram_tensor("lnb", [HFF], F32, kind="ExternalInput").ap()
    wpj_d = nc.dram_tensor("wpj", [D, HFF], F32, kind="ExternalInput").ap()
    id2_d = nc.dram_tensor("ident2", [128, 128], F32, kind="ExternalInput").ap()
    sel_d = nc.dram_tensor("selc", [2, 256], F32, kind="ExternalInput").ap()
    out_d = nc.dram_tensor("out", [T, D], F32, kind="ExternalOutput").ap()

    x1_kind = "ExternalOutput" if "x1d" in dumps else "Internal"
    x1_d = nc.dram_tensor("x1d", [T, D], F32, kind=x1_kind).ap()
    f8a_d = nc.dram_tensor("f8a", [128, NT, 8, 128], FP8, kind="Internal").ap()
    f8b_d = nc.dram_tensor("f8b", [128, 8, NT, 128], FP8, kind="Internal").ap()

    x_t = x_d.rearrange("(c t p) d -> c p t d", t=4, p=128)
    x1_t = x1_d.rearrange("(n p) d -> n p d", p=128)
    out_t = out_d.rearrange("(n p) d -> n p d", p=128)
    wq_r = wq_d.rearrange("(i p) j -> p i j", p=128)
    wo_r = wo_d.rearrange("(i p) j -> p i j", p=128)
    wkd_r = wkd_d.rearrange("(i p) j -> p i j", p=128)
    wvd_r = wvd_d.rearrange("(i p) j -> p i j", p=128)
    wfc_r = wfc_d.rearrange("(n p) d -> n p d", p=128)
    wpj_r = wpj_d.rearrange("(j p) h -> j p h", p=128)
    lnw_r = lnw_d.rearrange("(o h) -> o h", o=1)
    lnb_r = lnb_d.rearrange("(s p) -> s p", p=128)

    pl = Plan()
    dump_info = []

    with ExitStack() as st:
        en = st.enter_context
        semh = {s: en(nc.semaphore("sem_" + s)) for s in SEMS}

        def dump(name, ap, deps, cast=None):
            if name not in dumps:
                return
            dd = nc.dram_tensor("dbg_" + name, list(ap.shape), cast or ap.dtype,
                                kind="ExternalOutput").ap()
            dump_info.append(name)
            pl.emit('gpsimd', lambda e, ap=ap, dd=dd: e.dma_start(dd, ap),
                    deps=deps, inc='So0', k=16, tag='dump_' + name)

        cpool = en(nc.sbuf_tensor("cpool", [128, 6], F32))
        ident = en(nc.sbuf_tensor("ident", [128, 128], BF16))
        ones_c = en(nc.sbuf_tensor("ones_c", [128, 128], BF16))
        ones_r = en(nc.sbuf_tensor("ones_r", [1, 128], BF16))
        ln3mv = en(nc.sbuf_tensor("ln3mv", [128, NT, 2], F32))
        lnw_row = en(nc.sbuf_tensor("lnw_row", [1, HFF], BF16))
        lnb_sb = en(nc.sbuf_tensor("lnb_sb", [128, 32], BF16))
        bwq_sb = en(nc.sbuf_tensor("bwq_sb", [1, D], BF16))

        with ExitStack() as stA:
            ea = stA.enter_context
            idf32 = ea(nc.sbuf_tensor("idf32", [128, 128], F32))
            sel_sb = ea(nc.sbuf_tensor("sel_sb", [2, 256], F32))
            masks = ea(nc.sbuf_tensor("masks", [128, 4, CH], BF16))
            iotmp = ea(nc.sbuf_tensor("iotmp", [128, CH], F32))
            gacc = ea(nc.sbuf_tensor("gacc", [128, 2], F32))
            gtmp = ea(nc.sbuf_tensor("gtmp", [128, 2], F32))
            thrv = ea(nc.sbuf_tensor("thrv", [2, 4], F32))
            thr_b = ea(nc.sbuf_tensor("thr_b", [128, 4], F32))
            wq_b = ea(nc.sbuf_tensor("wq_b", [128, 8, D], BF16))
            wo_b = ea(nc.sbuf_tensor("wo_b", [128, 8, D], BF16))
            wkd_b = ea(nc.sbuf_tensor("wkd_b", [128, 8, 64], BF16))
            wvd_b = ea(nc.sbuf_tensor("wvd_b", [128, 8, 64], BF16))
            wku_b = ea(nc.sbuf_tensor("wku_b", [64, 128], BF16))
            wvu_b = ea(nc.sbuf_tensor("wvu_b", [64, 128], BF16))
            klatT = ea(nc.sbuf_tensor("klatT", [64, T], BF16))
            vlatT = ea(nc.sbuf_tensor("vlatT", [64, T], BF16))
            KT = ea(nc.sbuf_tensor("KT", [128, NKV, T], BF16))
            Vsb = ea(nc.sbuf_tensor("Vsb", [128, NKV, NT, HD], BF16))
            xc = ea(nc.sbuf_tensor("xc", [128, 2, 4, D], F32))
            x1 = ea(nc.sbuf_tensor("x1", [128, 4, D], BF16))
            x1T = ea(nc.sbuf_tensor("x1T", [128, 2, 8, CH], BF16))
            QT = ea(nc.sbuf_tensor("QT", [128, H, CH], BF16))
            ptb = ea(nc.sbuf_tensor("ptb", [128, 4, CH], BF16))
            yt = ea(nc.sbuf_tensor("yt", [128, H, CH], BF16))
            rb_sb = ea(nc.sbuf_tensor("rb_sb", [128, CH], BF16))
            rs_f = ea(nc.sbuf_tensor("rs_f", [1, CH], F32))
            rs_b = ea(nc.sbuf_tensor("rs_b", [1, 2, CH], BF16))
            asb = ea(nc.sbuf_tensor("asb", [128, 2, D], F32))
            mvln1 = ea(nc.sbuf_tensor("mvln1", [128, 4, 2], F32))
            rln1 = ea(nc.sbuf_tensor("rln1", [128, 4], F32))
            rtmp1 = ea(nc.sbuf_tensor("rtmp1", [128, 4], F32))
            mvln2 = ea(nc.sbuf_tensor("mvln2", [128, 2], F32))
            rln2 = ea(nc.sbuf_tensor("rln2", [128, 1], F32))
            rtmp2 = ea(nc.sbuf_tensor("rtmp2", [128, 1], F32))
            bn6 = ea(nc.sbuf_tensor("bn6", [128, 12], F32))
            bn6b = ea(nc.sbuf_tensor("bn6b", [128, 12], F32))
            lnb32 = ea(nc.sbuf_tensor("lnb32", [32, 128], F32))
            wstr = ea(nc.sbuf_tensor("wstr", [128, 2, D], F32))
            qa_b = ea(nc.sbuf_tensor("qa_b", [128, D], BF16))
            qn_b = ea(nc.sbuf_tensor("qn_b", [128, 2, D], BF16))
            f8ev = ea(nc.sbuf_tensor("f8ev", [128, 8, 128], FP8))

            ps_a = ea(nc.psum_tensor("ps_a", [128, 2, CH], F32))
            ps_b = ea(nc.psum_tensor("ps_b", [128, 2, CH], F32))
            ps_y = ea(nc.psum_tensor("ps_y", [128, CH], F32))
            ps_wo = ea(nc.psum_tensor("ps_wo", [128, 2, CH], F32))
            ps_m = ea(nc.psum_tensor("ps_m", [128, 8, 128], BF16))

            gv = 0
            for i, val in [(0, LN_EPS), (1, -0.5), (2, ATT_SCALE), (3, -10.0),
               (4, -1.0)]:
                gv = pl.emit('gpsimd', lambda e, i=i, v=val: e.memset(cpool[:, i:i + 1], v),
                             inc='G', tag=f'cp{i}')
            gv = pl.emit('gpsimd', lambda e: e.memset(ones_c[:, :], 0.0), inc='G', tag='co0')
            gv = pl.emit('gpsimd', lambda e: e.memset(ones_c[:, 0:1], 1.0),
                         deps=[('G', gv)], inc='G', tag='co')
            gv = pl.emit('gpsimd', lambda e: e.memset(ones_r[:, :], 1.0), inc='G', tag='cr')
            gv = pl.emit('gpsimd', lambda e: e.memset(gacc[:, :], 0.0), inc='G', tag='cg')
            mprev = 0
            for r in range(4):
                g1 = pl.emit('gpsimd',
                             lambda e, r=r: e.iota(iotmp[:, :], [[1, CH]], base=-128 * r,
                                                   channel_multiplier=-1,
                                                   allow_small_or_imprecise_dtypes=True),
                             deps=[('V', mprev)], inc='G', tag=f'iota{r}')
                mprev = pl.emit('vector',
                        lambda e, r=r: e.tensor_scalar(masks[:, r, :], iotmp[:, :],
                                                       0.0, None, ALU.is_ge),
                        deps=[('G', g1), ('V', pl.c('V'))], inc='V', tag=f'mask{r}')
            dump('masks', masks[:, :, :], [('V', pl.c('V'))])
            ev_masks = pl.c('V')
            ev_consts = pl.c('G')

            for nm, dst, src in [('wq', wq_b[:, :, :], wq_r),
                                 ('wo', wo_b[:, :, :], wo_r),
                                 ('wkd', wkd_b[:, :, :], wkd_r),
                                 ('wvd', wvd_b[:, :, :], wvd_r),
                                 ('wku0', wku_b[0:32, :], wku_d),
                                 ('wku1', wku_b[32:64, :], wku_d),
                                 ('wvu0', wvu_b[0:32, :], wvu_d),
                                 ('wvu1', wvu_b[32:64, :], wvu_d),
                                 ('id', ident[:, :], id2_d),
                                 ('lnw', lnw_row[:, :], lnw_r)]:
                pl.emit('gpsimd', lambda e, d=dst, s=src: e.dma_start(d, s),
                        inc='Lw', k=16, tag='ld_' + nm)
            pl.emit('sync', lambda e: e.dma_start(idf32[:, :], id2_d), inc='Lf', k=16,
                    tag='ld_idf')
            pl.emit('sync', lambda e: e.dma_start(sel_sb[:, :], sel_d), inc='Lf', k=16,
                    tag='ld_sel')
            pl.emit('sync', lambda e: e.dma_start(lnb32[:, :], lnb_r), inc='Lf', k=16,
                    tag='ld_lnb')
            ev_wload = pl.c('Lw')
            ev_fload = pl.c('Lf')
            tlb = pl.emit('tensor',
                          lambda e: e.transpose(ps_a[:, 0, 0:32], lnb32[:, :],
                                                idf32[0:32, 0:32]),
                          deps=[('Lw', ev_wload), ('Lf', ev_fload)], inc='T', tag='lnbT')
            slb = pl.emit('scalar', lambda e: e.copy(lnb_sb[:, :], ps_a[:, 0, 0:32]),
                          deps=[('T', tlb)], inc='S', tag='lnbE')

            ev_xload = [None] * NCH
            ev_chunk_resid = [0] * NCH
            ev_x1trT = [0]
            ev_x1T = [0]
            ps_slot = [('S', slb), ('S', 0), ('S', 0), ('S', 0)]
            ps_m_ev = [0]
            pt_T = [0, 0, 0, 0]
            ev_psy_free = [0, 0]
            ev_b0rd = [('S', 0)]
            ev_b1rd = [('S', 0)]
            ev_woev = [('S', 0)]
            ev_wo_lastT = [0]
            ev_QTev = [0]
            ev_asb_free = [0, 0]
            ev_asb_ln3 = [0, 0]
            ev_x1_store = [0] * NT
            ev_r1b = [0]
            ev_r2b = [0]
            wstr_free = [('V', 0), ('V', 0)]
            ev_f8ev_free = [0]
            ev_qnb_free = [('T', 0), ('T', 0)]
            last_gadd = [0, 0]
            sV_prev_gadd = lambda mat: last_gadd[mat]
            ev_thr = [0]

            qunits = [(0, n) for n in range(NT)] + [(1, j * 4 + c)
                                                    for j in range(8) for c in range(4)]

            def unit_src(u):
                mat, idx = qunits[u]
                if mat == 0:
                    return wfc_r[idx]
                j, c = idx // 4, idx % 4
                return wpj_r[j][:, c * D:(c + 1) * D]

            def gen_gamma_unit(u):
                sl = u % 2
                mat = qunits[u][0]
                sx = f'Ls{sl}'
                ld = pl.emit('sync', lambda e, u=u, sl=sl: e.dma_start(wstr[:, sl, :], unit_src(u)),
                             deps=[wstr_free[sl]], inc=sx, k=16, tag=f'gld{u}')
                sR = pl.emit('vector',
                             lambda e, sl=sl, mat=mat: e.tensor_reduce(
                                 gtmp[:, mat:mat + 1], wstr[:, sl, :], AX.X, ALU.add,
                                 apply_absolute_value=True),
                             deps=[(sx, ld), ('V', pl.c('V'))], inc='V',
                             tag=f'gabs{u}')
                sV = pl.emit('vector',
                             lambda e, mat=mat: e.tensor_tensor(
                                 gacc[:, mat:mat + 1], gacc[:, mat:mat + 1],
                                 gtmp[:, mat:mat + 1], ALU.add),
                             deps=[('V', sV_prev_gadd(mat)), ('V', pl.c('V'))],
                             inc='V', tag=f'gadd{u}')
                last_gadd[mat] = sV
                wstr_free[sl] = ('V', sR)
                return
                yield

            def plan_thr():
                tT = pl.emit('tensor',
                             lambda e: e.transpose(ps_a[0:2, 0, 0:128], gacc[:, :],
                                                   idf32[:, :]),
                             deps=[('V', pl.c('V')), ps_slot[0], ('Lf', ev_fload)],
                             inc='T', tag='thrT')
                v1 = pl.emit('vector',
                             lambda e: e.tensor_reduce(thrv[0:2, 0:1], ps_a[0:2, 0, 0:128],
                                                       AX.X, ALU.add),
                             deps=[('T', tT)], inc='V', tag='thrR')
                ps_slot[0] = ('V', v1)
                v2 = pl.emit('vector',
                             lambda e: e.tensor_scalar(thrv[0:2, 2:3], thrv[0:2, 0:1],
                                                       0.5 / (HFF * D), 0.5 * ABS_EPS,
                                                       ALU.mult, ALU.add),
                             deps=[('V', v1)], inc='V', tag='thrS')
                v3 = pl.emit('vector',
                             lambda e: e.tensor_scalar(thrv[0:2, 3:4], thrv[0:2, 2:3],
                                                       -1.0, None, ALU.mult),
                             deps=[('V', v2)], inc='V', tag='thrN')
                sE = 0
                for m in range(2):
                    tB = pl.emit('tensor',
                                 lambda e, m=m: e.matmul(ps_a[:, 1, 0:2],
                                                         sel_sb[:, m * 128:(m + 1) * 128],
                                                         thrv[0:2, 2:4],
                                                         start=True, stop=True),
                                 deps=[('V', v3), ps_slot[1], ('Lf', ev_fload)],
                                 inc='T', tag=f'thrB{m}')
                    sE = pl.emit('scalar',
                                 lambda e, m=m: e.copy(thr_b[:, 2 * m:2 * m + 2],
                                                       ps_a[:, 1, 0:2]),
                                 deps=[('T', tB)], inc='S', tag=f'thrE{m}')
                    ps_slot[1] = ('S', sE)
                ev_thr[0] = sE
                dump('gacc', gacc[:, :], [('V', pl.c('V'))])
                dump('thrv', thrv[:, :], [('V', pl.c('V'))])
                dump('thrb', thr_b[:, :], [('S', sE)])

            def gen_quant_unit(u):
                mat, idx = qunits[u]
                sl = u % 2
                tc, ncc = (0, 1) if mat == 0 else (2, 3)
                sx = f'Ls{sl}'
                ld = pl.emit('sync', lambda e, u=u, sl=sl: e.dma_start(wstr[:, sl, :], unit_src(u)),
                             deps=[wstr_free[sl]], inc=sx, k=16, tag=f'qld{u}')
                v1 = pl.emit('vector',
                             lambda e, sl=sl, tc=tc: e.tensor_scalar(
                                 qa_b[:, :], wstr[:, sl, :], thr_b[:, tc:tc + 1], None,
                                 ALU.is_gt),
                             deps=[(sx, ld), ('S', ev_thr[0]), ('V', pl.c('V'))],
                             inc='V', tag=f'q1_{u}')
                yield
                qs = u % 2
                v2 = pl.emit('vector',
                             lambda e, sl=sl, ncc=ncc, qs=qs: e.scalar_tensor_tensor(
                                 qn_b[:, qs, :], wstr[:, sl, :], thr_b[:, ncc:ncc + 1],
                                 qa_b[:, :], ALU.is_lt, ALU.subtract),
                             deps=[('V', v1), ev_qnb_free[qs]], inc='V', tag=f'q2_{u}')
                wstr_free[sl] = ('V', v2)
                yield
                tl = 0
                for i in range(8):
                    tl = pl.emit('tensor',
                                 lambda e, i=i, qs=qs: e.transpose(
                                     ps_m[:, i, :],
                                     qn_b[:, qs, i * 128:(i + 1) * 128], ident[:, :]),
                                 deps=[('V', v2), ('V', ps_m_ev[0])],
                                 inc=('T' if i == 7 else None), tag=f'qT{u}_{i}')
                ev_qnb_free[qs] = ('T', tl)
                yield
                v3 = pl.emit('vector',
                             lambda e: e.tensor_scalar(f8ev[:, :, :], ps_m[:, :, :],
                                                       -1.0, None, ALU.mult),
                             deps=[('T', tl), ('Sf8', ev_f8ev_free[0])],
                             inc='V', tag=f'qE{u}')
                ps_m_ev[0] = v3
                if mat == 0:
                    dst = f8a_d[:, idx, :, :]
                else:
                    j, c = idx // 4, idx % 4
                    dst = f8b_d[:, j, c * 8:(c + 1) * 8, :]
                ev_f8ev_free[0] = pl.emit(
                    'gpsimd', lambda e, dst=dst: e.dma_start(dst, f8ev[:, :, :]),
                    deps=[('V', v3)], inc='Sf8', k=16, tag=f'qS{u}')

            ev_xload[0] = ('Lx0', pl.emit('sync',
                                  lambda e: e.dma_start(xc[:, 0, :, :], x_t[0]),
                                  inc='Lx0', k=16, tag='x0'))

            ev_x1tr = [0] * 4
            ev_x1T_ch = [0, 0]
            ev_ln1rd = [('S', 0)] * 4
            ev_rsb_rd = [0, 0]

            def gen_ln1_item(qc2, t):
                bufq = qc2 % 2
                pl.emit('vector',
                        lambda e, t=t, bufq=bufq: e.bn_stats(bn6[:, 0:6],
                                                             xc[:, bufq, t, 0:512]),
                        deps=[ev_xload[qc2], ev_ln1rd[t], ('V', pl.c('V'))],
                        inc='V', tag=f'l1a{qc2}_{t}')
                b2 = pl.emit('vector',
                             lambda e, t=t, bufq=bufq: e.bn_stats(bn6[:, 6:12],
                                                                  xc[:, bufq, t, 512:1024]),
                             deps=[], inc='V', tag=f'l1b{qc2}_{t}')
                yield
                ag = pl.emit('vector',
                             lambda e, t=t: e.bn_aggr(mvln1[:, t, :], bn6[:, :]),
                             deps=[('V', b2)], inc='V', tag=f'l1c{qc2}_{t}')
                s1 = pl.emit('scalar',
                             lambda e, t=t: e.activation(rtmp1[:, t:t + 1],
                                                         mvln1[:, t, 1:2], AF.Ln,
                                                         bias=cpool[:, 0:1]),
                             deps=[('V', ag), ('G', ev_consts), ('S', pl.c('S'))],
                             inc='S', tag=f'r1a{qc2}_{t}')
                s2 = pl.emit('scalar',
                             lambda e, t=t: e.activation(rln1[:, t:t + 1],
                                                         rtmp1[:, t:t + 1], AF.Exp,
                                                         scale=cpool[:, 1:2]),
                             deps=[('S', s1)], inc='S', tag=f'r1b{qc2}_{t}')
                ev_ln1rd[t] = ('S', s2)
                yield
                v1 = pl.emit('vector',
                             lambda e, t=t, bufq=bufq: e.tensor_scalar(
                                 x1[:, t, :], xc[:, bufq, t, :], mvln1[:, t, 0:1],
                                 rln1[:, t:t + 1], ALU.subtract, ALU.mult),
                             deps=[('S', s2), ('T', ev_x1tr[t])], inc='V',
                             tag=f'x1_{qc2}_{t}')
                yield
                tl = 0
                for i in range(8):
                    tl = pl.emit('tensor',
                                 lambda e, t=t, i=i: e.transpose(
                                     ps_m[:, i, :], x1[:, t, i * 128:(i + 1) * 128],
                                     ident[:, :]),
                                 deps=[('V', v1), ('V', ps_m_ev[0]),
                                       ('Lw', ev_wload)],
                                 inc=('T' if i == 7 else None),
                                 tag=f'x1T{qc2}_{t}_{i}')
                ev_x1tr[t] = tl
                ps_m_ev[0] = pl.emit(
                    'vector',
                    lambda e, t=t, bufq=bufq: e.tensor_copy(
                        x1T[:, bufq, :, t * 128:(t + 1) * 128], ps_m[:, :, :]),
                    deps=[('T', tl)], inc='V', tag=f'x1Te{qc2}_{t}')
                ev_x1T_ch[bufq] = ps_m_ev[0]

            for t in range(4):
                for _ in gen_ln1_item(0, t):
                    pass

            for qc in range(NCH):
                buf = qc % 2
                if qc + 1 < NCH:
                    dep = [('V', ev_chunk_resid[qc - 1])] if qc >= 1 else []
                    sx = f'Lx{(qc + 1) % 2}'
                    ev_xload[qc + 1] = (sx, pl.emit(
                        'sync', lambda e, q=qc + 1: e.dma_start(xc[:, q % 2, :, :], x_t[q]),
                        deps=dep, inc=sx, k=16, tag=f'x{qc+1}'))
                if qc == 5:
                    plan_thr()

                ev_x1T_v = ev_x1T_ch[buf]
                ev_lat = 0
                for which, wsrc, dstT in [(0, wkd_b, klatT), (1, wvd_b, vlatT)]:
                    tl = 0
                    for i in range(8):
                        r = pl.emit('tensor',
                                    lambda e, i=i, w=wsrc, which=which, buf=buf: e.matmul(
                                        ps_a[0:64, which, :], w[:, i, :],
                                        x1T[:, buf, i, :],
                                        start=(i == 0), stop=(i == 7)),
                                    deps=[('V', ev_x1T_v), ps_slot[which]],
                                    inc=('T' if i == 7 else None),
                                    tag=f'lat{which}_{qc}_{i}')
                        tl = r if r is not None else tl
                    sE = pl.emit('scalar',
                                 lambda e, d=dstT, qc=qc, which=which: e.copy(
                                     d[:, qc * CH:(qc + 1) * CH], ps_a[0:64, which, :]),
                                 deps=[('T', tl)], inc='S', tag=f'late{which}_{qc}')
                    ps_slot[which] = ('S', sE)
                    ev_lat = sE
                for g in range(NKV):
                    tK = pl.emit('tensor',
                                 lambda e, g=g, qc=qc: e.matmul(
                                     ps_b[:, 0, :], wku_b[32 * g:32 * (g + 1), :],
                                     klatT[32 * g:32 * (g + 1), qc * CH:(qc + 1) * CH],
                                     start=True, stop=True),
                                 deps=[('S', ev_lat), ps_slot[2]], inc='T',
                                 tag=f'KT{qc}_{g}')
                    sE = pl.emit('scalar',
                                 lambda e, g=g, qc=qc: e.copy(
                                     KT[:, g, qc * CH:(qc + 1) * CH], ps_b[:, 0, :]),
                                 deps=[('T', tK)], inc='S', tag=f'KTe{qc}_{g}')
                    ps_slot[2] = ('S', sE)
                    tl = 0
                    for t in range(4):
                        kt = qc * 4 + t
                        tl = pl.emit('tensor',
                                     lambda e, g=g, t=t, kt=kt: e.matmul(
                                         ps_b[:, 1, t * 128:(t + 1) * 128],
                                         vlatT[32 * g:32 * (g + 1),
                                               kt * 128:(kt + 1) * 128],
                                         wvu_b[32 * g:32 * (g + 1), :],
                                         start=True, stop=True),
                                     deps=[('S', ev_lat), ps_slot[3]], inc='T',
                                     tag=f'V{qc}_{g}_{t}')
                    sE = pl.emit('scalar',
                                 lambda e, g=g, qc=qc: e.copy(
                                     Vsb[:, g, qc * 4:(qc + 1) * 4, :]
                                     .rearrange("p t d -> p (t d)"), ps_b[:, 1, :]),
                                 deps=[('T', tl)], inc='S', tag=f'Ve{qc}_{g}')
                    ps_slot[3] = ('S', sE)
                ev_KV = pl.c('S')
                dump(f'KT{qc}', KT[:, :, :], [('S', ev_KV)])
                dump(f'V{qc}', Vsb[:, :, :, :], [('S', ev_KV)])
                for h in range(H):
                    slot = h % 4
                    pst = [ps_a[:, 0, :], ps_a[:, 1, :], ps_b[:, 0, :], ps_b[:, 1, :]][slot]
                    tl = 0
                    for i in range(8):
                        r = pl.emit('tensor',
                                    lambda e, h=h, i=i, pst=pst, buf=buf: e.matmul(
                                        pst, wq_b[:, i, h * 128:(h + 1) * 128],
                                        x1T[:, buf, i, :], start=(i == 0), stop=(i == 7)),
                                    deps=[('V', ev_x1T_v), ps_slot[slot]],
                                    inc=('T' if i == 7 else None),
                                    tag=f'QT{qc}_{h}_{i}')
                        tl = r if r is not None else tl
                    sE = pl.emit('scalar',
                                 lambda e, h=h, pst=pst: e.copy(QT[:, h, :], pst),
                                 deps=[('T', tl)], inc='S', tag=f'QTe{qc}_{h}')
                    ps_slot[slot] = ('S', sE)
                ev_QTev[0] = pl.c('S')
                dump(f'QT{qc}', QT[:, :, :], [('S', ev_QTev[0])])

                side = []
                if qc + 1 < NCH:
                    for t in range(4):
                        side.append(gen_ln1_item(qc + 1, t))
                gamma_sched = {2: range(0, 28), 3: range(28, 64)}
                quant_sched = {5: range(0, 20), 6: range(20, 42),
                               7: range(42, 64)}
                for u in gamma_sched.get(qc, []):
                    side.append(gen_gamma_unit(u))
                for u in quant_sched.get(qc, []):
                    side.append(gen_quant_unit(u))
                nstage = sum(1 for _ in ()) or None

                nk = 4 * (qc + 1)
                tasks = [(h, kt) for h in range(H) for kt in range(nk)]
                nstages = (4 * (4 if qc + 1 < NCH else 0)
                           + len(gamma_sched.get(qc, []))
                           + 3 * len(quant_sched.get(qc, [])))
                drain_every = max(1, len(tasks) // (nstages + 1))
                qk_ev = {}
                sslots = [ps_a[:, 0, :], ps_a[:, 1, :], ps_b[:, 0, :], ps_b[:, 1, :]]

                def emit_qk(i):
                    h, kt = tasks[i]
                    g = h // 4
                    sl4 = i % 4
                    qk_ev[i] = pl.emit('tensor',
                                       lambda e, g=g, h=h, kt=kt, sl4=sl4: e.matmul(
                                           sslots[sl4],
                                           KT[:, g, kt * 128:(kt + 1) * 128],
                                           QT[:, h, :], start=True, stop=True),
                                       deps=[('S', ev_QTev[0]), ps_slot[sl4],
                                             ('S', ev_KV)],
                                       inc='T', tag=f'qk{qc}_{h}_{kt}')

                pend_ep = []

                def emit_ep_start(h, tl):
                    l1 = pl.emit('scalar',
                                 lambda e: e.activation(rs_f[:, :], ps_wo[0:1, 0, :],
                                                        AF.Ln),
                                 deps=[('T', tl), ('S', pl.c('S'))], inc='S',
                                 tag=f'rc{qc}_{h}')
                    ev_b0rd[0] = ('S', l1)
                    l2 = pl.emit('scalar',
                                 lambda e, h=h: e.activation(rs_b[:, h % 2, :],
                                                             rs_f[:, :], AF.Exp,
                                                             scale=cpool[0:1, 4:5]),
                                 deps=[('S', l1), ('T', ev_rsb_rd[h % 2])], inc='S',
                                 tag=f'rcb{qc}_{h}')
                    return (h, tl, l2)

                def emit_ep_finish(h, tl, l2, cur_i):
                    bslot = (cur_i - 1) % 4
                    tb = pl.emit('tensor',
                                 lambda e, bslot=bslot, h=h: e.matmul(
                                     sslots[bslot], ones_r[:, :],
                                     rs_b[:, h % 2, :], start=True, stop=True),
                                 deps=[('S', l2), ps_slot[bslot], ev_b1rd[0]],
                                 inc='T', tag=f'bc{qc}_{h}')
                    ev_rsb_rd[h % 2] = tb
                    sE = pl.emit('vector',
                                 lambda e, bslot=bslot: e.tensor_copy(
                                     rb_sb[:, :], sslots[bslot]),
                                 deps=[('T', tb), ('V', ev_psy_free[h % 2]),
                                       ('V', pl.c('V'))],
                                 inc='V', tag=f'rbe{qc}_{h}')
                    ps_slot[bslot] = ('V', sE)
                    ev_b1rd[0] = ('V', sE)
                    psy = ps_y[:, :] if h % 2 == 0 else ps_wo[:, 1, :]
                    v3 = pl.emit('vector',
                                 lambda e, h=h, psy=psy: e.tensor_tensor(
                                     yt[:, h, :], psy, rb_sb[:, :], ALU.mult),
                                 deps=[('V', sE), ('T', tl), ('T', ev_wo_lastT[0]),
                                       ('V', pl.c('V'))],
                                 inc='V', tag=f'yt{qc}_{h}')
                    ev_psy_free[h % 2] = v3

                emit_qk(0)
                emit_qk(1)
                emit_qk(2)
                for i, (h, kt) in enumerate(tasks):
                    g = h // 4
                    sl4 = i % 4
                    tqk = qk_ev[i]
                    sE = pl.emit('scalar',
                                 lambda e, sl4=sl4: e.activation(
                                     ptb[:, sl4, :], sslots[sl4], AF.Exp,
                                     bias=cpool[:, 3:4], scale=cpool[:, 2:3]),
                                 deps=[('T', tqk), ('T', pt_T[sl4]),
                                       ('G', ev_consts)],
                                 inc='S', tag=f'exp{qc}_{h}_{kt}')
                    ps_slot[sl4] = ('S', sE)
                    vm = 0
                    r = kt - 4 * qc
                    if r >= 0:
                        vm = pl.emit('vector',
                                     lambda e, sl4=sl4, r=r: e.tensor_tensor(
                                         ptb[:, sl4, :], ptb[:, sl4, :],
                                         masks[:, r, :], ALU.mult),
                                     deps=[('S', sE), ('V', ev_masks),
                                           ('V', pl.c('V'))],
                                     inc='V', tag=f'msk{qc}_{h}_{kt}')
                    if pend_ep and i >= pend_ep[0][0]:
                        emit_ep_finish(*pend_ep.pop(0)[1], i)
                    if i + 3 < len(tasks):
                        emit_qk(i + 3)
                    pdep = [('V', vm)] if vm else [('S', sE)]
                    pl.emit('tensor',
                            lambda e, sl4=sl4, stt=(kt == 0): e.matmul(
                                ps_wo[:, 0, :], ones_c[:, :],
                                ptb[:, sl4, :], start=stt, stop=True,
                                skip_group_check=True),
                            deps=pdep + [ev_b0rd[0], ev_woev[0]],
                            tag=f'ss{qc}_{h}_{kt}')
                    psy = ps_y[:, :] if h % 2 == 0 else ps_wo[:, 1, :]
                    tl = pl.emit('tensor',
                                 lambda e, g=g, kt=kt, sl4=sl4, stt=(kt == 0),
                                        psy=psy: e.matmul(
                                     psy, Vsb[:, g, kt, :],
                                     ptb[:, sl4, :], start=stt, stop=True,
                                     skip_group_check=True),
                                 deps=[('V', ev_psy_free[h % 2])] + pdep,
                                 inc='T', tag=f'pv{qc}_{h}_{kt}')
                    pt_T[sl4] = tl
                    if kt == nk - 1:
                        pend_ep.append((i + 5, emit_ep_start(h, tl)))
                    if side and (i + 1) % drain_every == 0:
                        try:
                            next(side[0])
                        except StopIteration:
                            side.pop(0)
                while side:
                    try:
                        next(side[0])
                    except StopIteration:
                        side.pop(0)
                while pend_ep:
                    emit_ep_finish(*pend_ep.pop(0)[1], len(tasks) + 4)
                ev_yt_all = pl.c('V')
                dump(f'yt{qc}', yt[:, :, :], [('V', ev_yt_all)])
                for t in range(4):
                    tile = qc * 4 + t
                    slot = t % 2
                    pool = [ps_wo, ps_a, ps_b][t % 3]
                    if t % 3 == 0:
                        pdeps = [ev_woev[0], ev_b0rd[0], ('V', ev_psy_free[1])]
                    elif t % 3 == 1:
                        pdeps = [ps_slot[0], ps_slot[1]]
                    else:
                        pdeps = [ps_slot[2], ps_slot[3]]
                    tl = 0
                    for h in range(H):
                        for j in range(2):
                            r = pl.emit('tensor',
                                        lambda e, h=h, j=j, t=t, pool=pool: e.matmul(
                                            pool[:, j, :],
                                            yt[:, h, t * 128:(t + 1) * 128],
                                            wo_b[:, h, j * 512:(j + 1) * 512],
                                            start=(h == 0), stop=(h == H - 1)),
                                        deps=[('V', ev_yt_all)] + pdeps,
                                        inc=('T' if h == H - 1 else None),
                                        tag=f'wo{qc}_{t}_{h}_{j}')
                            tl = r if r is not None else tl
                    ev_wo_lastT[0] = tl
                    sE = pl.emit('vector',
                                 lambda e, slot=slot, pool=pool: e.tensor_copy(
                                     asb[:, slot, :],
                                     pool[:, :, :].rearrange("p j c -> p (j c)")),
                                 deps=[('T', tl),
                                       ('Sxa' if slot == 0 else 'Sxb', ev_asb_free[slot]),
                                       ('V', ev_asb_ln3[slot]), ('V', pl.c('V'))],
                                 inc='V', tag=f'woe{qc}_{t}')
                    if t % 3 == 0:
                        ev_woev[0] = ('V', sE)
                        ev_psy_free[1] = sE
                    elif t % 3 == 1:
                        ps_slot[0] = ('V', sE)
                        ps_slot[1] = ('V', sE)
                    else:
                        ps_slot[2] = ('V', sE)
                        ps_slot[3] = ('V', sE)
                    pl.emit('vector',
                            lambda e, slot=slot: e.bn_stats(bn6b[:, 0:6],
                                                            asb[:, slot, 0:512]),
                            deps=[('V', sE), ('S', ev_r2b[0]), ('V', pl.c('V'))],
                            inc='V', tag=f'l2a{qc}_{t}')
                    b2 = pl.emit('vector',
                                 lambda e, slot=slot: e.bn_stats(bn6b[:, 6:12],
                                                                 asb[:, slot, 512:1024]),
                                 deps=[], inc='V', tag=f'l2b{qc}_{t}')
                    b3 = pl.emit('vector',
                                 lambda e: e.bn_aggr(mvln2[:, :], bn6b[:, :]),
                                 deps=[('V', b2)], inc='V', tag=f'l2c{qc}_{t}')
                    s1 = pl.emit('scalar',
                                 lambda e: e.activation(rtmp2[:, :], mvln2[:, 1:2], AF.Ln,
                                                        bias=cpool[:, 0:1]),
                                 deps=[('V', b3)], inc='S', tag=f'l2d{qc}_{t}')
                    s2 = pl.emit('scalar',
                                 lambda e: e.activation(rln2[:, :], rtmp2[:, :], AF.Exp,
                                                        scale=cpool[:, 1:2]),
                                 deps=[('S', s1)], inc='S', tag=f'l2e{qc}_{t}')
                    ev_r2b[0] = s2
                    v1 = pl.emit('vector',
                                 lambda e, slot=slot: e.tensor_scalar(
                                     asb[:, slot, :], asb[:, slot, :], mvln2[:, 0:1],
                                     rln2[:, 0:1], ALU.subtract, ALU.mult),
                                 deps=[('S', s2), ('V', pl.c('V'))], inc='V',
                                 tag=f'l2f{qc}_{t}')
                    v2 = pl.emit('vector',
                                 lambda e, slot=slot, t=t, buf=buf: e.tensor_tensor(
                                     asb[:, slot, :], asb[:, slot, :], xc[:, buf, t, :],
                                     ALU.add),
                                 deps=[('V', v1)], inc='V', tag=f'res{qc}_{t}')
                    if t == 3:
                        ev_chunk_resid[qc] = v2
                    pl.emit('vector',
                            lambda e, slot=slot: e.bn_stats(bn6b[:, 0:6],
                                                            asb[:, slot, 0:512]),
                            deps=[('V', v2)], inc='V', tag=f'l3a{qc}_{t}')
                    b2 = pl.emit('vector',
                                 lambda e, slot=slot: e.bn_stats(bn6b[:, 6:12],
                                                                 asb[:, slot, 512:1024]),
                                 deps=[], inc='V', tag=f'l3b{qc}_{t}')
                    b3 = pl.emit('vector',
                                 lambda e, tile=tile: e.bn_aggr(ln3mv[:, tile, :],
                                                                bn6b[:, :]),
                                 deps=[('V', b2)], inc='V', tag=f'l3c{qc}_{t}')
                    ev_asb_ln3[slot] = b3
                    ev_x1_store[tile] = pl.emit(
                        'gpsimd',
                        lambda e, slot=slot, tile=tile: e.dma_start(x1_t[tile],
                                                                    asb[:, slot, :]),
                        deps=[('V', v2)], inc=('Sxa' if slot == 0 else 'Sxb'),
                        k=16, tag=f'x1s{qc}_{t}')
                    ev_asb_free[slot] = ev_x1_store[tile]


            endA = {}
            endA['T'] = pl.emit('tensor', lambda e: e.nop(), inc='T', tag='fT')
            endA['V'] = pl.emit('vector', lambda e: e.nop(), inc='V', tag='fV')
            endA['S'] = pl.emit('scalar', lambda e: e.nop(), inc='S', tag='fS')
            endA['G'] = pl.emit('gpsimd', lambda e: e.nop(), inc='G', tag='fG')
            endA['Sxa'] = pl.c('Sxa')
            endA['Sxb'] = pl.c('Sxb')
            endA['Sf8'] = pl.c('Sf8')
            fence = [('T', endA['T']), ('V', endA['V']), ('S', endA['S']),
                     ('G', endA['G'])]
            pl.flush(nc, semh)

        with ExitStack() as stB:
            eb = stB.enter_context
            wfc8 = eb(nc.sbuf_tensor("wfc8", [128, NT, 8, 128], FP8))
            wp8 = eb(nc.sbuf_tensor("wp8", [128, 8, NT, 128], FP8))
            w_rep = eb(nc.sbuf_tensor("w_rep", [128, HFF], BF16))
            x1t = eb(nc.sbuf_tensor("x1t", [128, 8, D], F32))
            h_g = eb(nc.sbuf_tensor("h_g", [128, 2, HFF], F32))
            zb = eb(nc.sbuf_tensor("zb", [128, 2, HFF], BF16))
            x2 = eb(nc.sbuf_tensor("x2", [128, 2, D], BF16))
            x2T = eb(nc.sbuf_tensor("x2T", [128, 2, 8, 128], BF16))
            gsum = eb(nc.sbuf_tensor("gsum", [128, 2, 8], F32))
            qsum = eb(nc.sbuf_tensor("qsum", [128, 2, 8], F32))
            sqd = eb(nc.sbuf_tensor("sqd", [128, 512], BF16))
            stt1 = eb(nc.sbuf_tensor("stt1", [128, 4], F32))
            zT = eb(nc.sbuf_tensor("zT", [128, NT, 128], BF16))
            o2sb = eb(nc.sbuf_tensor("o2sb", [128, 2, D], F32))
            bwq_rep = eb(nc.sbuf_tensor("bwq_rep", [128, D], F32))
            outb = eb(nc.sbuf_tensor("outb", [128, 2, D], F32))
            r3all = eb(nc.sbuf_tensor("r3all", [128, NT], F32))
            r3tmp = eb(nc.sbuf_tensor("r3tmp", [128, NT], F32))
            hmv = eb(nc.sbuf_tensor("hmv", [128, 4, 2], F32))
            rhh = eb(nc.sbuf_tensor("rhh", [128, 2], F32))
            rht = eb(nc.sbuf_tensor("rht", [128, 2], F32))
            o4mv = eb(nc.sbuf_tensor("o4mv", [128, 2, 2], F32))
            r44 = eb(nc.sbuf_tensor("r44", [128, 2], F32))
            r4t = eb(nc.sbuf_tensor("r4t", [128, 2], F32))
            bnB = eb(nc.sbuf_tensor("bnB", [128, 12], F32))

            ps_h = eb(nc.psum_tensor("ps_h", [128, 2, 512], F32))
            ps_ztp = eb(nc.psum_tensor("ps_ztp", [128, 2, 16, 128], BF16))
            ps_o2 = eb(nc.psum_tensor("ps_o2", [128, 2, 512], F32))

            pl.emit('sync', lambda e: e.dma_start(wfc8[:, :, :, :], f8a_d),
                    deps=[('Sf8', endA['Sf8']), ('Sxa', endA['Sxa']),
                          ('Sxb', endA['Sxb'])] + fence,
                    inc='L8', k=16, tag='ldf8a')
            ldB = pl.emit('sync', lambda e: e.dma_start(wp8[:, :, :, :], f8b_d),
                          deps=[], inc='L8', k=16, tag='ldf8b')
            ev_w8 = ('L8', ldB)
            ldA = ldB

            ev_wr = 0
            for c in range(8):
                tb = pl.emit('tensor',
                             lambda e, c=c: e.matmul(ps_o2[:, 0, :], ones_r[:, :],
                                                     lnw_row[0:1, c * 512:(c + 1) * 512],
                                                     start=True, stop=True),
                             deps=fence + [('V', ev_wr), ('Lw', ev_wload)],
                             inc='T', tag=f'wr{c}')
                ev_wr = pl.emit('vector',
                                lambda e, c=c: e.tensor_copy(
                                    w_rep[:, c * 512:(c + 1) * 512], ps_o2[:, 0, :]),
                                deps=[('T', tb)], inc='V', tag=f'wre{c}')

            s1 = pl.emit('scalar',
                         lambda e: e.activation(r3tmp[:, :], ln3mv[:, :, 1], AF.Ln,
                                                bias=cpool[:, 0:1]),
                         deps=fence, inc='S', tag='r3a')
            ev_r3 = pl.emit('scalar',
                            lambda e: e.activation(r3all[:, :], r3tmp[:, :], AF.Exp,
                                                   scale=cpool[:, 1:2]),
                            deps=[('S', s1)], inc='S', tag='r3b')

            tl = 0
            for dc in range(2):
                for s in range(NT):
                    r = pl.emit('tensor',
                                lambda e, dc=dc, s=s: e.matmul(
                                    ps_h[0:1, dc, :], lnb_sb[:, s:s + 1],
                                    wp8[:, 4 * dc:4 * dc + 4, s, :],
                                    start=(s == 0), stop=(s == NT - 1)),
                                deps=[ev_w8, ('Lw', ev_wload)],
                                inc=('T' if s == NT - 1 else None),
                                tag=f'bwq{dc}_{s}')
                    tl = r if r is not None else tl
            ev_bwq = pl.emit('scalar',
                             lambda e: e.copy(bwq_sb[:, :],
                                              ps_h[0:1, :, :].rearrange("p j c -> p (j c)")),
                             deps=[('T', tl)], inc='S', tag='bwqe')
            ev_brep = 0
            for dc in range(2):
                tb = pl.emit('tensor',
                             lambda e, dc=dc: e.matmul(ps_o2[:, dc, :], ones_r[:, :],
                                                       bwq_sb[0:1, dc * 512:(dc + 1) * 512],
                                                       start=True, stop=True),
                             deps=[('S', ev_bwq), ('V', ev_wr)], inc='T',
                             tag=f'brep{dc}')
                ev_brep = pl.emit('vector',
                                  lambda e, dc=dc: e.tensor_copy(
                                      bwq_rep[:, dc * 512:(dc + 1) * 512],
                                      ps_o2[:, dc, :]),
                                  deps=[('T', tb)], inc='V', tag=f'brepe{dc}')
            ev_o2_rd0 = ev_brep

            dump('wfc8', wfc8[:, :, :, :], [('L8', ldA)], cast=F32)
            dump('wp8', wp8[:, :, :, :], [('L8', ldB)], cast=F32)
            dump('bwq', bwq_sb[:, :], [('S', ev_bwq)], cast=F32)
            dump('wrep', w_rep[:, :], [('V', ev_wr)], cast=F32)
            ev_x1load = [None] * NT
            ev_x1slot_free = [0] * 8
            ev_hslot_free = [0, 0]
            ev_x2T_ev = [0]
            ev_fc_last = [0]
            ev_zT_free = [0]
            ev_o2sb_free = [0, 0]
            ev_o2e = [ev_wr, ev_wr]
            ev_o2_rd = [None]
            ev_outb_free = [0, 0]
            ev_ztp_ev = [0]
            ev_gelu = [0, 0]
            ev_hmv = [0, 0, 0, 0]
            ev_o4mv = [0, 0]
            ev_rh = [0]
            ev_r4 = [0]
            pend = []

            for t in range(4):
                sx = f'Lb{t % 8}'
                ev_x1load[t] = (sx, pl.emit('sync',
                                       lambda e, t=t: e.dma_start(x1t[:, t, :], x1_t[t]),
                                       deps=[('Sxa', endA['Sxa']),
                                             ('Sxb', endA['Sxb'])] + fence,
                                       inc=sx, k=16, tag=f'bx{t}'))

            def ln4_apply(tile):
                par = tile % 2
                slot = tile % 8
                v1 = pl.emit('vector',
                             lambda e, par=par: e.tensor_scalar(
                                 o2sb[:, par, :], o2sb[:, par, :], o4mv[:, par, 0:1],
                                 r44[:, par:par + 1], ALU.subtract, ALU.mult),
                             deps=[('S', ev_r4[0]), ('V', ev_o2e[par]),
                                   ('V', pl.c('V'))],
                             inc='V', tag=f'l4f{tile}')
                v2 = pl.emit('vector',
                             lambda e, par=par, slot=slot: e.tensor_tensor(
                                 outb[:, par, :], o2sb[:, par, :], x1t[:, slot, :],
                                 ALU.add),
                             deps=[('V', v1), (f'So{par}', ev_outb_free[par])],
                             inc='V', tag=f'l4r{tile}')
                if tile == 0:
                    dump('ap_o4mv', o4mv[:, :, :], [('S', ev_r4[0])])
                    dump('ap_r44', r44[:, :], [('S', ev_r4[0])])
                    dump('ap_o2sb', o2sb[:, 0, :], [('V', v1)])
                    dump('ap_x1t', x1t[:, 0, :], [('V', v2)])
                    dump('ap_outb', outb[:, 0, :], [('V', v2)])
                ev_x1slot_free[slot] = v2
                ev_o2sb_free[par] = v2
                ev_outb_free[par] = pl.emit(
                    'gpsimd',
                    lambda e, par=par, tile=tile: e.dma_start(out_t[tile],
                                                              outb[:, par, :]),
                    deps=[('V', v2)], inc=f'So{par}', k=16, tag=f'outs{tile}')

            ev_ztp = [0, 0]
            ev_o2_rd[0] = ev_o2_rd0
            ev_r44_rd = [0]

            ev_x2st = [('V', 0), ('V', 0)]

            def plan_x2stage(tile):
                xb = tile % 2
                v1 = pl.emit('vector',
                             lambda e, tile=tile, slot=tile % 8, xb=xb: e.tensor_scalar(
                                 x2[:, xb, :], x1t[:, slot, :], ln3mv[:, tile, 0:1],
                                 r3all[:, tile:tile + 1], ALU.subtract, ALU.mult),
                             deps=[ev_x1load[tile], ('S', ev_r3),
                                   ('T', ev_fc_last[0]), ('V', pl.c('V'))],
                             inc='V', tag=f'x2_{tile}')
                tl = 0
                for i in range(8):
                    tl = pl.emit('tensor',
                                 lambda e, i=i, xb=xb: e.transpose(
                                     ps_ztp[:, 0, i, :],
                                     x2[:, xb, i * 128:(i + 1) * 128], ident[:, :]),
                                 deps=[('V', v1), ('V', ev_ztp[0])],
                                 inc=('T' if i == 7 else None), tag=f'x2T{tile}_{i}')
                vE = pl.emit('vector',
                             lambda e, xb=xb: e.tensor_copy(x2T[:, xb, :, :],
                                                            ps_ztp[:, 0, 0:8, :]),
                             deps=[('T', tl)], inc='V', tag=f'x2Te{tile}')
                ev_ztp[0] = vE
                ev_x2st[xb] = ('V', vE)

            def plan_fc(tile):
                par = tile % 2
                xb = tile % 2
                for hc in range(8):
                    pb = hc % 2
                    tl = 0
                    for i in range(8):
                        r = pl.emit('tensor',
                                    lambda e, hc=hc, i=i, pb=pb, xb=xb: e.matmul(
                                        ps_h[:, pb, :], x2T[:, xb, i, :],
                                        wfc8[:, 4 * hc:4 * hc + 4, i, :],
                                        start=(i == 0), stop=(i == 7)),
                                    deps=[ev_x2st[xb], ev_w8,
                                          ('S', ev_gelu[pb]), ('S', ev_bwq)],
                                    inc=('T' if i == 7 else None),
                                    tag=f'fc{tile}_{hc}_{i}')
                        tl = r if r is not None else tl
                    ev_gelu[pb] = pl.emit(
                        'scalar',
                        lambda e, hc=hc, pb=pb, par=par: e.activation(
                            h_g[:, par, hc * 512:(hc + 1) * 512], ps_h[:, pb, :],
                            AF.Gelu, accum_out=gsum[:, par, hc:hc + 1]),
                        deps=[('T', tl), ('V', ev_hslot_free[par])],
                        inc='S', tag=f'gel{tile}_{hc}')
                    pl.emit('scalar',
                            lambda e, hc=hc, par=par: e.activation(
                                sqd[:, :], h_g[:, par, hc * 512:(hc + 1) * 512],
                                AF.Square, accum_out=qsum[:, par, hc:hc + 1]),
                            deps=[('S', ev_gelu[pb])], inc='S', tag=f'sq{tile}_{hc}')
                ev_fc_last[0] = tl
                ev_sq = pl.c('S')
                v1 = pl.emit('vector',
                             lambda e, par=par: e.tensor_reduce(
                                 stt1[:, 0:1], gsum[:, par, :], AX.X, ALU.add),
                             deps=[('S', ev_sq), ('V', pl.c('V'))], inc='V',
                             tag=f'st1{tile}')
                v2 = pl.emit('vector',
                             lambda e, par=par: e.tensor_reduce(
                                 stt1[:, 1:2], qsum[:, par, :], AX.X, ALU.add),
                             deps=[], inc='V', tag=f'st2{tile}')
                h4 = tile % 4
                v3 = pl.emit('vector',
                             lambda e, h4=h4: e.tensor_scalar(
                                 hmv[:, h4, 0:1], stt1[:, 0:1], 1.0 / HFF, None,
                                 ALU.mult),
                             deps=[('V', v2), ('S', ev_rh[0])], inc='V',
                             tag=f'st3{tile}')
                v4 = pl.emit('vector',
                             lambda e, h4=h4: e.tensor_tensor(
                                 stt1[:, 2:3], hmv[:, h4, 0:1],
                                 hmv[:, h4, 0:1], ALU.mult),
                             deps=[('V', v3)], inc='V', tag=f'st4{tile}')
                ev_hmv[h4] = pl.emit('vector',
                                     lambda e, h4=h4: e.scalar_tensor_tensor(
                                         hmv[:, h4, 1:2], stt1[:, 1:2], 1.0 / HFF,
                                         stt1[:, 2:3], ALU.mult, ALU.subtract),
                                     deps=[('V', v4)], inc='V', tag=f'st5{tile}')
                if tile <= 1:
                    dump(f'hmv{tile}', hmv[:, :, :], [('V', ev_hmv[h4])])
                    dump(f'hg{tile}', h_g[:, tile % 2, :], [('V', ev_hmv[h4])])

            def plan_rstd_h(g):
                c0 = (2 * g) % 4
                s1 = pl.emit('scalar',
                             lambda e, c0=c0: e.activation(rht[:, :],
                                                           hmv[:, c0:c0 + 2, 1],
                                                           AF.Ln, bias=cpool[:, 0:1]),
                             deps=[('V', ev_hmv[c0]), ('V', ev_hmv[c0 + 1]),
                                   ('V', pl.c('V'))],
                             inc='S', tag=f'rha{g}')
                ev_rh[0] = pl.emit('scalar',
                                   lambda e: e.activation(rhh[:, :], rht[:, :], AF.Exp,
                                                          scale=cpool[:, 1:2]),
                                   deps=[('S', s1)], inc='S', tag=f'rhb{g}')

            def plan_r4(g):
                s3 = pl.emit('scalar',
                             lambda e: e.activation(r4t[:, :], o4mv[:, :, 1], AF.Ln,
                                                    bias=cpool[:, 0:1]),
                             deps=[('V', ev_o4mv[0]), ('V', ev_o4mv[1]),
                                   ('V', ev_r44_rd[0])],
                             inc='S', tag=f'r4a{g}')
                ev_r4[0] = pl.emit('scalar',
                                   lambda e: e.activation(r44[:, :], r4t[:, :], AF.Exp,
                                                          scale=cpool[:, 1:2]),
                                   deps=[('S', s3)], inc='S', tag=f'r4b{g}')

            def plan_applies():
                for pt_ in pend:
                    ln4_apply(pt_)
                if pend:
                    ev_r44_rd[0] = pl.c('V')
                pend.clear()

            ev_zbfree = [('T', 0), ('T', 0)]

            def plan_z(tt):
                pz = tt % 2
                v1 = pl.emit('vector',
                             lambda e, pz=pz, h4=tt % 4: e.tensor_scalar(
                                 h_g[:, pz, :], h_g[:, pz, :], hmv[:, h4, 0:1],
                                 None, ALU.subtract),
                             deps=[('V', ev_hmv[tt % 4]), ('V', pl.c('V'))],
                             inc='V', tag=f'z1_{tt}')
                v2 = pl.emit('vector',
                             lambda e, pz=pz: e.tensor_tensor(
                                 zb[:, pz, :], h_g[:, pz, :], w_rep[:, :], ALU.mult),
                             deps=[('V', v1), ev_zbfree[pz], ('V', ev_wr)],
                             inc='V', tag=f'z2_{tt}')
                ev_hslot_free[pz] = v2
                return v2

            def plan_ztproj(tt, v2):
                pz = tt % 2
                for w in range(2):
                    tl = 0
                    for i in range(16):
                        s_ = w * 16 + i
                        tl = pl.emit('tensor',
                                     lambda e, s_=s_, i=i, w=w, pz=pz: e.transpose(
                                         ps_ztp[:, w, i, :],
                                         zb[:, pz, s_ * 128:(s_ + 1) * 128],
                                         ident[:, :]),
                                     deps=[('V', v2), ('V', ev_ztp[w])],
                                     inc=('T' if i == 15 else None),
                                     tag=f'zT{tt}_{s_}')
                    ev_ztp[w] = pl.emit(
                        'vector',
                        lambda e, w=w: e.tensor_copy(
                            zT[:, w * 16:(w + 1) * 16, :], ps_ztp[:, w, :, :]),
                        deps=[('T', tl)], inc='V', tag=f'zTe{tt}_{w}')
                ev_zbfree[pz] = ('T', tl)
                tl = 0
                for dc in range(2):
                    for s_ in range(NT):
                        r = pl.emit('tensor',
                                    lambda e, dc=dc, s_=s_: e.matmul(
                                        ps_o2[:, dc, :], zT[:, s_, :],
                                        wp8[:, 4 * dc:4 * dc + 4, s_, :],
                                        start=(s_ == 0), stop=(s_ == NT - 1)),
                                    deps=[('V', ev_ztp[0]), ('V', ev_ztp[1]),
                                          ('V', ev_o2_rd[0])],
                                    inc=('T' if s_ == NT - 1 else None),
                                    tag=f'pj{tt}_{dc}_{s_}')
                        tl = r if r is not None else tl
                ev_zT_free[0] = tl
                v4 = pl.emit('vector',
                             lambda e, pz=pz: e.scalar_tensor_tensor(
                                 o2sb[:, pz, :],
                                 ps_o2[:, :, :].rearrange("p j c -> p (j c)"),
                                 rhh[:, pz:pz + 1], bwq_rep[:, :],
                                 ALU.mult, ALU.add),
                             deps=[('T', tl), ('V', ev_o2sb_free[pz]),
                                   ('S', ev_rh[0]), ('V', ev_brep)],
                             inc='V', tag=f'o2e{tt}')
                ev_o2e[pz] = v4
                ev_o2_rd[0] = v4
                if tt == 0:
                    dump('o2sb0', o2sb[:, 0, :], [('V', v4)])
                    dump('rhh0', rhh[:, :], [('V', v4)])
                b1 = pl.emit('vector',
                             lambda e, pz=pz: e.bn_stats(bnB[:, 0:6], o2sb[:, pz, 0:512]),
                             deps=[('V', v4)], inc='V', tag=f'l4a{tt}')
                b2 = pl.emit('vector',
                             lambda e, pz=pz: e.bn_stats(bnB[:, 6:12],
                                                         o2sb[:, pz, 512:1024]),
                             deps=[], inc='V', tag=f'l4b{tt}')
                b3 = pl.emit('vector',
                             lambda e, pz=pz: e.bn_aggr(o4mv[:, pz, :], bnB[:, :]),
                             deps=[('V', b2)], inc='V', tag=f'l4c{tt}')
                ev_o4mv[pz] = b3
                pend.append(tt)

            plan_x2stage(0)
            for tile in range(NT):
                if tile + 1 < NT:
                    plan_x2stage(tile + 1)
                zpair = None
                if tile >= 2 and tile % 2 == 0:
                    plan_applies()
                    zpair = (plan_z(tile - 2), plan_z(tile - 1))
                plan_fc(tile)
                if zpair is not None:
                    plan_ztproj(tile - 2, zpair[0])
                    plan_ztproj(tile - 1, zpair[1])
                if tile % 2 == 1:
                    plan_rstd_h((tile - 1) // 2)
                    if tile >= 3:
                        plan_r4((tile - 3) // 2)
                if tile + 4 < NT:
                    nt_ = tile + 4
                    sx = f'Lb{nt_ % 8}'
                    ev_x1load[nt_] = (sx, pl.emit(
                        'sync',
                        lambda e, nt_=nt_: e.dma_start(x1t[:, nt_ % 8, :], x1_t[nt_]),
                        deps=[('Sxa', endA['Sxa']), ('Sxb', endA['Sxb']),
                              ('V', ev_x1slot_free[nt_ % 8])],
                        inc=sx, k=16, tag=f'bx{nt_}'))

            plan_applies()
            za = plan_z(NT - 2)
            zbv = plan_z(NT - 1)
            plan_ztproj(NT - 2, za)
            plan_ztproj(NT - 1, zbv)
            plan_r4(NT // 2 - 1)
            plan_applies()

            pl.emit('gpsimd', lambda e: None,
                    deps=[('So0', pl.c('So0')), ('So1', pl.c('So1'))], tag='final')
            pl.flush(nc, semh)

    return nc, dump_info


def kernel(L0_attn_x, L0_attn_wq, L0_attn_wk_down, L0_attn_wv_down,
           L0_attn_wk_up, L0_attn_wv_up, L0_attn_wo,
           L0_ffn_W_fc, L0_ffn_ln_w, L0_ffn_ln_b, L0_ffn_W_proj, _dumps=()):
    f = lambda a: np.ascontiguousarray(np.asarray(a, dtype=np.float32))
    x = f(L0_attn_x)
    sel = np.zeros((2, 256), dtype=np.float32)
    sel[0, 0:128] = 1.0
    sel[1, 128:256] = 1.0
    ws = {
        "wq": f(L0_attn_wq), "wkd": f(L0_attn_wk_down), "wvd": f(L0_attn_wv_down),
        "wku": f(L0_attn_wk_up), "wvu": f(L0_attn_wv_up), "wo": f(L0_attn_wo),
        "wfc": f(L0_ffn_W_fc), "lnw": f(L0_ffn_ln_w), "lnb": f(L0_ffn_ln_b),
        "wpj": f(L0_ffn_W_proj),
        "ident2": np.eye(128, dtype=np.float32),
        "selc": sel,
    }
    nc, _ = build_program(dumps=_dumps)
    in_maps = [dict(ws, x=x[i]) for i in range(8)]
    res = run_bass_kernel_spmd(nc, in_maps, core_ids=list(range(8)))
    out = np.stack([res.results[i]["out"] for i in range(8)], axis=0)
    if _dumps:
        kernel._last_results = res
    return out.reshape(8, T, D)


# version 30
# speedup vs baseline: 1.0648x; 1.0102x over previous
import numpy as np
from contextlib import ExitStack

import concourse.bass as bass
import concourse.mybir as mybir
from concourse.bass_utils import run_bass_kernel_spmd

F32 = mybir.dt.float32
BF16 = mybir.dt.bfloat16
FP8 = mybir.dt.float8e4
AF = mybir.ActivationFunctionType
ALU = mybir.AluOpType
AX = mybir.AxisListType

T, D, H, HD, NKV, DLAT, HFF = 4096, 1024, 8, 128, 2, 32, 4096
NT, NCH, CH = 32, 8, 512
LN_EPS = 1e-5
ATT_SCALE = float(1.0 / np.sqrt(128.0))
ABS_EPS = 1e-6

ENGINES = ('sync', 'gpsimd', 'vector', 'scalar', 'tensor')
SEMS = ('T', 'V', 'S', 'G', 'Lx0', 'Lx1', 'Lw', 'Lf', 'Ls0', 'Ls1', 'L8',
        'Lb0', 'Lb1', 'Lb2', 'Lb3', 'Lb4', 'Lb5', 'Lb6', 'Lb7',
        'Sxa', 'Sxb', 'Sf8', 'So0', 'So1')


class Plan:

    def __init__(self):
        self.ops = {e: [] for e in ENGINES}
        self.cnt = {s: 0 for s in SEMS}
        self.waited = {e: {} for e in ENGINES}

    def c(self, sem):
        return self.cnt[sem]

    def emit(self, eng, fn, deps=(), inc=None, k=1, tag=''):
        dl = []
        for (s, v) in deps:
            if v <= 0:
                continue
            assert v <= self.cnt[s], f"dep ahead of plan: {tag}: {s} {v} > {self.cnt[s]}"
            if self.waited[eng].get(s, 0) < v:
                dl.append((s, v))
                self.waited[eng][s] = v
        self.ops[eng].append((tuple(dl), fn, inc, k, tag))
        if inc is not None:
            self.cnt[inc] += k
            return self.cnt[inc]
        return None

    def flush(self, nc, semh):
        ops = {e: self.ops[e] for e in ENGINES}
        self.ops = {e: [] for e in ENGINES}
        with nc.Block() as blk:
            for eng in ENGINES:
                if not ops[eng]:
                    continue

                def body(e, _ops=ops[eng]):
                    for (dl, fn, inc, k, tag) in _ops:
                        for (s, v) in dl:
                            e.wait_ge(semh[s], v)
                        ins = fn(e)
                        if inc is not None:
                            assert ins is not None, f"{tag}: no instruction to inc"
                            ins.then_inc(semh[inc], k)
                getattr(blk, eng)(body)


def build_program(dumps=()):
    nc = bass.Bass("TRN2", target_bir_lowering=False)

    x_d = nc.dram_tensor("x", [T, D], F32, kind="ExternalInput").ap()
    wq_d = nc.dram_tensor("wq", [D, D], F32, kind="ExternalInput").ap()
    wkd_d = nc.dram_tensor("wkd", [D, NKV * DLAT], F32, kind="ExternalInput").ap()
    wvd_d = nc.dram_tensor("wvd", [D, NKV * DLAT], F32, kind="ExternalInput").ap()
    wku_d = nc.dram_tensor("wku", [DLAT, HD], F32, kind="ExternalInput").ap()
    wvu_d = nc.dram_tensor("wvu", [DLAT, HD], F32, kind="ExternalInput").ap()
    wo_d = nc.dram_tensor("wo", [D, D], F32, kind="ExternalInput").ap()
    wfc_d = nc.dram_tensor("wfc", [HFF, D], F32, kind="ExternalInput").ap()
    lnw_d = nc.dram_tensor("lnw", [HFF], F32, kind="ExternalInput").ap()
    lnb_d = nc.dram_tensor("lnb", [HFF], F32, kind="ExternalInput").ap()
    wpj_d = nc.dram_tensor("wpj", [D, HFF], F32, kind="ExternalInput").ap()
    id2_d = nc.dram_tensor("ident2", [128, 128], F32, kind="ExternalInput").ap()
    sel_d = nc.dram_tensor("selc", [2, 256], F32, kind="ExternalInput").ap()
    out_d = nc.dram_tensor("out", [T, D], F32, kind="ExternalOutput").ap()

    x1_kind = "ExternalOutput" if "x1d" in dumps else "Internal"
    x1_d = nc.dram_tensor("x1d", [T, D], F32, kind=x1_kind).ap()
    f8a_d = nc.dram_tensor("f8a", [128, NT, 8, 128], FP8, kind="Internal").ap()
    f8b_d = nc.dram_tensor("f8b", [128, 8, NT, 128], FP8, kind="Internal").ap()

    x_t = x_d.rearrange("(c t p) d -> c p t d", t=4, p=128)
    x1_t = x1_d.rearrange("(n p) d -> n p d", p=128)
    out_t = out_d.rearrange("(n p) d -> n p d", p=128)
    wq_r = wq_d.rearrange("(i p) j -> p i j", p=128)
    wo_r = wo_d.rearrange("(i p) j -> p i j", p=128)
    wkd_r = wkd_d.rearrange("(i p) j -> p i j", p=128)
    wvd_r = wvd_d.rearrange("(i p) j -> p i j", p=128)
    wfc_r = wfc_d.rearrange("(n p) d -> n p d", p=128)
    wpj_r = wpj_d.rearrange("(j p) h -> j p h", p=128)
    lnw_r = lnw_d.rearrange("(o h) -> o h", o=1)
    lnb_r = lnb_d.rearrange("(s p) -> s p", p=128)

    pl = Plan()
    dump_info = []

    with ExitStack() as st:
        en = st.enter_context
        semh = {s: en(nc.semaphore("sem_" + s)) for s in SEMS}

        def dump(name, ap, deps, cast=None):
            if name not in dumps:
                return
            dd = nc.dram_tensor("dbg_" + name, list(ap.shape), cast or ap.dtype,
                                kind="ExternalOutput").ap()
            dump_info.append(name)
            pl.emit('gpsimd', lambda e, ap=ap, dd=dd: e.dma_start(dd, ap),
                    deps=deps, inc='So0', k=16, tag='dump_' + name)

        cpool = en(nc.sbuf_tensor("cpool", [128, 6], F32))
        ident = en(nc.sbuf_tensor("ident", [128, 128], BF16))
        ones_c = en(nc.sbuf_tensor("ones_c", [128, 128], BF16))
        ones_r = en(nc.sbuf_tensor("ones_r", [1, 128], BF16))
        ln3mv = en(nc.sbuf_tensor("ln3mv", [128, NT, 2], F32))
        lnw_row = en(nc.sbuf_tensor("lnw_row", [1, HFF], BF16))
        lnb_sb = en(nc.sbuf_tensor("lnb_sb", [128, 32], BF16))
        bwq_sb = en(nc.sbuf_tensor("bwq_sb", [1, D], BF16))

        with ExitStack() as stA:
            ea = stA.enter_context
            idf32 = ea(nc.sbuf_tensor("idf32", [128, 128], F32))
            sel_sb = ea(nc.sbuf_tensor("sel_sb", [2, 256], F32))
            masks = ea(nc.sbuf_tensor("masks", [128, 4, CH], BF16))
            iotmp = ea(nc.sbuf_tensor("iotmp", [128, CH], F32))
            gacc = ea(nc.sbuf_tensor("gacc", [128, 2], F32))
            gtmp = ea(nc.sbuf_tensor("gtmp", [128, 2], F32))
            thrv = ea(nc.sbuf_tensor("thrv", [2, 4], F32))
            thr_b = ea(nc.sbuf_tensor("thr_b", [128, 4], F32))
            wq_b = ea(nc.sbuf_tensor("wq_b", [128, 8, D], BF16))
            wo_b = ea(nc.sbuf_tensor("wo_b", [128, 8, D], BF16))
            wkd_b = ea(nc.sbuf_tensor("wkd_b", [128, 8, 64], BF16))
            wvd_b = ea(nc.sbuf_tensor("wvd_b", [128, 8, 64], BF16))
            wku_b = ea(nc.sbuf_tensor("wku_b", [64, 128], BF16))
            wvu_b = ea(nc.sbuf_tensor("wvu_b", [64, 128], BF16))
            klatT = ea(nc.sbuf_tensor("klatT", [64, T], BF16))
            vlatT = ea(nc.sbuf_tensor("vlatT", [64, T], BF16))
            KT = ea(nc.sbuf_tensor("KT", [128, NKV, T], BF16))
            Vsb = ea(nc.sbuf_tensor("Vsb", [128, NKV, NT, HD], BF16))
            xc = ea(nc.sbuf_tensor("xc", [128, 2, 4, D], F32))
            x1 = ea(nc.sbuf_tensor("x1", [128, 4, D], BF16))
            x1T = ea(nc.sbuf_tensor("x1T", [128, 2, 8, CH], BF16))
            QT = ea(nc.sbuf_tensor("QT", [128, H, CH], BF16))
            ptb = ea(nc.sbuf_tensor("ptb", [128, 4, CH], BF16))
            yt = ea(nc.sbuf_tensor("yt", [128, H, CH], BF16))
            rb_sb = ea(nc.sbuf_tensor("rb_sb", [128, CH], BF16))
            rs_f = ea(nc.sbuf_tensor("rs_f", [1, CH], F32))
            rs_b = ea(nc.sbuf_tensor("rs_b", [128, 2, CH], BF16))
            asb = ea(nc.sbuf_tensor("asb", [128, 2, D], F32))
            mvln1 = ea(nc.sbuf_tensor("mvln1", [128, 4, 2], F32))
            rln1 = ea(nc.sbuf_tensor("rln1", [128, 4], F32))
            rtmp1 = ea(nc.sbuf_tensor("rtmp1", [128, 4], F32))
            mvln2 = ea(nc.sbuf_tensor("mvln2", [128, 2], F32))
            rln2 = ea(nc.sbuf_tensor("rln2", [128, 1], F32))
            rtmp2 = ea(nc.sbuf_tensor("rtmp2", [128, 1], F32))
            bn6 = ea(nc.sbuf_tensor("bn6", [128, 12], F32))
            bn6b = ea(nc.sbuf_tensor("bn6b", [128, 12], F32))
            lnb32 = ea(nc.sbuf_tensor("lnb32", [32, 128], F32))
            wstr = ea(nc.sbuf_tensor("wstr", [128, 2, D], F32))
            qa_b = ea(nc.sbuf_tensor("qa_b", [128, D], BF16))
            qn_b = ea(nc.sbuf_tensor("qn_b", [128, 2, D], BF16))
            f8ev = ea(nc.sbuf_tensor("f8ev", [128, 8, 128], FP8))

            ps_a = ea(nc.psum_tensor("ps_a", [128, 2, CH], F32))
            ps_b = ea(nc.psum_tensor("ps_b", [128, 2, CH], F32))
            ps_y = ea(nc.psum_tensor("ps_y", [128, CH], F32))
            ps_wo = ea(nc.psum_tensor("ps_wo", [128, 2, CH], F32))
            ps_m = ea(nc.psum_tensor("ps_m", [128, 8, 128], BF16))

            gv = 0
            for i, val in [(0, LN_EPS), (1, -0.5), (2, ATT_SCALE), (3, -10.0),
               (4, -1.0)]:
                gv = pl.emit('gpsimd', lambda e, i=i, v=val: e.memset(cpool[:, i:i + 1], v),
                             inc='G', tag=f'cp{i}')
            gv = pl.emit('gpsimd', lambda e: e.memset(ones_c[:, :], 0.0), inc='G', tag='co0')
            gv = pl.emit('gpsimd', lambda e: e.memset(ones_c[:, 0:1], 1.0),
                         deps=[('G', gv)], inc='G', tag='co')
            gv = pl.emit('gpsimd', lambda e: e.memset(ones_c[0:1, :], 1.0),
                         deps=[('G', gv)], inc='G', tag='co1')
            gv = pl.emit('gpsimd', lambda e: e.memset(rs_b[:, :, :], 0.0),
                         deps=[('G', gv)], inc='G', tag='co2')
            gv = pl.emit('gpsimd', lambda e: e.memset(ones_r[:, :], 1.0), inc='G', tag='cr')
            gv = pl.emit('gpsimd', lambda e: e.memset(gacc[:, :], 0.0), inc='G', tag='cg')
            mprev = 0
            for r in range(4):
                g1 = pl.emit('gpsimd',
                             lambda e, r=r: e.iota(iotmp[:, :], [[1, CH]], base=-128 * r,
                                                   channel_multiplier=-1,
                                                   allow_small_or_imprecise_dtypes=True),
                             deps=[('V', mprev)], inc='G', tag=f'iota{r}')
                mprev = pl.emit('vector',
                        lambda e, r=r: e.tensor_scalar(masks[:, r, :], iotmp[:, :],
                                                       0.0, None, ALU.is_ge),
                        deps=[('G', g1), ('V', pl.c('V'))], inc='V', tag=f'mask{r}')
            dump('masks', masks[:, :, :], [('V', pl.c('V'))])
            ev_masks = pl.c('V')
            ev_consts = pl.c('G')

            for nm, dst, src in [('wq', wq_b[:, :, :], wq_r),
                                 ('wo', wo_b[:, :, :], wo_r),
                                 ('wkd', wkd_b[:, :, :], wkd_r),
                                 ('wvd', wvd_b[:, :, :], wvd_r),
                                 ('wku0', wku_b[0:32, :], wku_d),
                                 ('wku1', wku_b[32:64, :], wku_d),
                                 ('wvu0', wvu_b[0:32, :], wvu_d),
                                 ('wvu1', wvu_b[32:64, :], wvu_d),
                                 ('id', ident[:, :], id2_d),
                                 ('lnw', lnw_row[:, :], lnw_r)]:
                pl.emit('gpsimd', lambda e, d=dst, s=src: e.dma_start(d, s),
                        inc='Lw', k=16, tag='ld_' + nm)
            pl.emit('sync', lambda e: e.dma_start(idf32[:, :], id2_d), inc='Lf', k=16,
                    tag='ld_idf')
            pl.emit('sync', lambda e: e.dma_start(sel_sb[:, :], sel_d), inc='Lf', k=16,
                    tag='ld_sel')
            pl.emit('sync', lambda e: e.dma_start(lnb32[:, :], lnb_r), inc='Lf', k=16,
                    tag='ld_lnb')
            ev_wload = pl.c('Lw')
            ev_fload = pl.c('Lf')
            tlb = pl.emit('tensor',
                          lambda e: e.transpose(ps_a[:, 0, 0:32], lnb32[:, :],
                                                idf32[0:32, 0:32]),
                          deps=[('Lw', ev_wload), ('Lf', ev_fload)], inc='T', tag='lnbT')
            slb = pl.emit('scalar', lambda e: e.copy(lnb_sb[:, :], ps_a[:, 0, 0:32]),
                          deps=[('T', tlb)], inc='S', tag='lnbE')

            ev_xload = [None] * NCH
            ev_chunk_resid = [0] * NCH
            ev_x1trT = [0]
            ev_x1T = [0]
            ps_slot = [('S', slb), ('S', 0), ('S', 0), ('S', 0)]
            ps_m_ev = [0]
            pt_T = [0, 0, 0, 0]
            ev_psy_free = [0, 0]
            ev_b0rd = [('S', 0)]
            ev_b1rd = [('S', 0)]
            ev_woev = [('S', 0)]
            ev_wo_lastT = [0]
            ev_QTev = [0]
            ev_asb_free = [0, 0]
            ev_asb_ln3 = [0, 0]
            ev_x1_store = [0] * NT
            ev_r1b = [0]
            ev_r2b = [0]
            wstr_free = [('V', 0), ('V', 0)]
            ev_f8ev_free = [0]
            ev_qnb_free = [('T', 0), ('T', 0)]
            last_gadd = [0, 0]
            sV_prev_gadd = lambda mat: last_gadd[mat]
            ev_thr = [0]

            qunits = [(0, n) for n in range(NT)] + [(1, j * 4 + c)
                                                    for j in range(8) for c in range(4)]

            def unit_src(u):
                mat, idx = qunits[u]
                if mat == 0:
                    return wfc_r[idx]
                j, c = idx // 4, idx % 4
                return wpj_r[j][:, c * D:(c + 1) * D]

            def gen_gamma_unit(u):
                sl = u % 2
                mat = qunits[u][0]
                sx = f'Ls{sl}'
                ld = pl.emit('sync', lambda e, u=u, sl=sl: e.dma_start(wstr[:, sl, :], unit_src(u)),
                             deps=[wstr_free[sl]], inc=sx, k=16, tag=f'gld{u}')
                sR = pl.emit('vector',
                             lambda e, sl=sl, mat=mat: e.tensor_reduce(
                                 gtmp[:, mat:mat + 1], wstr[:, sl, :], AX.X, ALU.add,
                                 apply_absolute_value=True),
                             deps=[(sx, ld), ('V', pl.c('V'))], inc='V',
                             tag=f'gabs{u}')
                sV = pl.emit('vector',
                             lambda e, mat=mat: e.tensor_tensor(
                                 gacc[:, mat:mat + 1], gacc[:, mat:mat + 1],
                                 gtmp[:, mat:mat + 1], ALU.add),
                             deps=[('V', sV_prev_gadd(mat)), ('V', pl.c('V'))],
                             inc='V', tag=f'gadd{u}')
                last_gadd[mat] = sV
                wstr_free[sl] = ('V', sR)
                return
                yield

            def plan_thr():
                tT = pl.emit('tensor',
                             lambda e: e.transpose(ps_a[0:2, 0, 0:128], gacc[:, :],
                                                   idf32[:, :]),
                             deps=[('V', pl.c('V')), ps_slot[0], ('Lf', ev_fload)],
                             inc='T', tag='thrT')
                v1 = pl.emit('vector',
                             lambda e: e.tensor_reduce(thrv[0:2, 0:1], ps_a[0:2, 0, 0:128],
                                                       AX.X, ALU.add),
                             deps=[('T', tT)], inc='V', tag='thrR')
                ps_slot[0] = ('V', v1)
                v2 = pl.emit('vector',
                             lambda e: e.tensor_scalar(thrv[0:2, 2:3], thrv[0:2, 0:1],
                                                       0.5 / (HFF * D), 0.5 * ABS_EPS,
                                                       ALU.mult, ALU.add),
                             deps=[('V', v1)], inc='V', tag='thrS')
                v3 = pl.emit('vector',
                             lambda e: e.tensor_scalar(thrv[0:2, 3:4], thrv[0:2, 2:3],
                                                       -1.0, None, ALU.mult),
                             deps=[('V', v2)], inc='V', tag='thrN')
                sE = 0
                for m in range(2):
                    tB = pl.emit('tensor',
                                 lambda e, m=m: e.matmul(ps_a[:, 1, 0:2],
                                                         sel_sb[:, m * 128:(m + 1) * 128],
                                                         thrv[0:2, 2:4],
                                                         start=True, stop=True),
                                 deps=[('V', v3), ps_slot[1], ('Lf', ev_fload)],
                                 inc='T', tag=f'thrB{m}')
                    sE = pl.emit('scalar',
                                 lambda e, m=m: e.copy(thr_b[:, 2 * m:2 * m + 2],
                                                       ps_a[:, 1, 0:2]),
                                 deps=[('T', tB)], inc='S', tag=f'thrE{m}')
                    ps_slot[1] = ('S', sE)
                ev_thr[0] = sE
                dump('gacc', gacc[:, :], [('V', pl.c('V'))])
                dump('thrv', thrv[:, :], [('V', pl.c('V'))])
                dump('thrb', thr_b[:, :], [('S', sE)])

            def gen_quant_unit(u):
                mat, idx = qunits[u]
                sl = u % 2
                tc, ncc = (0, 1) if mat == 0 else (2, 3)
                sx = f'Ls{sl}'
                ld = pl.emit('sync', lambda e, u=u, sl=sl: e.dma_start(wstr[:, sl, :], unit_src(u)),
                             deps=[wstr_free[sl]], inc=sx, k=16, tag=f'qld{u}')
                v1 = pl.emit('vector',
                             lambda e, sl=sl, tc=tc: e.tensor_scalar(
                                 qa_b[:, :], wstr[:, sl, :], thr_b[:, tc:tc + 1], None,
                                 ALU.is_gt),
                             deps=[(sx, ld), ('S', ev_thr[0]), ('V', pl.c('V'))],
                             inc='V', tag=f'q1_{u}')
                yield
                qs = u % 2
                v2 = pl.emit('vector',
                             lambda e, sl=sl, ncc=ncc, qs=qs: e.scalar_tensor_tensor(
                                 qn_b[:, qs, :], wstr[:, sl, :], thr_b[:, ncc:ncc + 1],
                                 qa_b[:, :], ALU.is_lt, ALU.subtract),
                             deps=[('V', v1), ev_qnb_free[qs]], inc='V', tag=f'q2_{u}')
                wstr_free[sl] = ('V', v2)
                yield
                tl = 0
                for i in range(8):
                    tl = pl.emit('tensor',
                                 lambda e, i=i, qs=qs: e.transpose(
                                     ps_m[:, i, :],
                                     qn_b[:, qs, i * 128:(i + 1) * 128], ident[:, :]),
                                 deps=[('V', v2), ('V', ps_m_ev[0])],
                                 inc=('T' if i == 7 else None), tag=f'qT{u}_{i}')
                ev_qnb_free[qs] = ('T', tl)
                yield
                v3 = pl.emit('vector',
                             lambda e: e.tensor_scalar(f8ev[:, :, :], ps_m[:, :, :],
                                                       -1.0, None, ALU.mult),
                             deps=[('T', tl), ('Sf8', ev_f8ev_free[0])],
                             inc='V', tag=f'qE{u}')
                ps_m_ev[0] = v3
                if mat == 0:
                    dst = f8a_d[:, idx, :, :]
                else:
                    j, c = idx // 4, idx % 4
                    dst = f8b_d[:, j, c * 8:(c + 1) * 8, :]
                ev_f8ev_free[0] = pl.emit(
                    'gpsimd', lambda e, dst=dst: e.dma_start(dst, f8ev[:, :, :]),
                    deps=[('V', v3)], inc='Sf8', k=16, tag=f'qS{u}')

            ev_xload[0] = ('Lx0', pl.emit('sync',
                                  lambda e: e.dma_start(xc[:, 0, :, :], x_t[0]),
                                  inc='Lx0', k=16, tag='x0'))

            ev_x1tr = [0] * 4
            ev_x1T_ch = [0, 0]
            ev_ln1rd = [('S', 0)] * 4
            ev_rsb_rd = [0, 0]

            def gen_ln1_item(qc2, t):
                bufq = qc2 % 2
                pl.emit('vector',
                        lambda e, t=t, bufq=bufq: e.bn_stats(bn6[:, 0:6],
                                                             xc[:, bufq, t, 0:512]),
                        deps=[ev_xload[qc2], ev_ln1rd[t], ('V', pl.c('V'))],
                        inc='V', tag=f'l1a{qc2}_{t}')
                b2 = pl.emit('vector',
                             lambda e, t=t, bufq=bufq: e.bn_stats(bn6[:, 6:12],
                                                                  xc[:, bufq, t, 512:1024]),
                             deps=[], inc='V', tag=f'l1b{qc2}_{t}')
                yield
                ag = pl.emit('vector',
                             lambda e, t=t: e.bn_aggr(mvln1[:, t, :], bn6[:, :]),
                             deps=[('V', b2)], inc='V', tag=f'l1c{qc2}_{t}')
                s1 = pl.emit('scalar',
                             lambda e, t=t: e.activation(rtmp1[:, t:t + 1],
                                                         mvln1[:, t, 1:2], AF.Ln,
                                                         bias=cpool[:, 0:1]),
                             deps=[('V', ag), ('G', ev_consts), ('S', pl.c('S'))],
                             inc='S', tag=f'r1a{qc2}_{t}')
                s2 = pl.emit('scalar',
                             lambda e, t=t: e.activation(rln1[:, t:t + 1],
                                                         rtmp1[:, t:t + 1], AF.Exp,
                                                         scale=cpool[:, 1:2]),
                             deps=[('S', s1)], inc='S', tag=f'r1b{qc2}_{t}')
                ev_ln1rd[t] = ('S', s2)
                yield
                v1 = pl.emit('vector',
                             lambda e, t=t, bufq=bufq: e.tensor_scalar(
                                 x1[:, t, :], xc[:, bufq, t, :], mvln1[:, t, 0:1],
                                 rln1[:, t:t + 1], ALU.subtract, ALU.mult),
                             deps=[('S', s2), ('T', ev_x1tr[t])], inc='V',
                             tag=f'x1_{qc2}_{t}')
                yield
                tl = 0
                for i in range(8):
                    tl = pl.emit('tensor',
                                 lambda e, t=t, i=i: e.transpose(
                                     ps_m[:, i, :], x1[:, t, i * 128:(i + 1) * 128],
                                     ident[:, :]),
                                 deps=[('V', v1), ('V', ps_m_ev[0]),
                                       ('Lw', ev_wload)],
                                 inc=('T' if i == 7 else None),
                                 tag=f'x1T{qc2}_{t}_{i}')
                ev_x1tr[t] = tl
                ps_m_ev[0] = pl.emit(
                    'vector',
                    lambda e, t=t, bufq=bufq: e.tensor_copy(
                        x1T[:, bufq, :, t * 128:(t + 1) * 128], ps_m[:, :, :]),
                    deps=[('T', tl)], inc='V', tag=f'x1Te{qc2}_{t}')
                ev_x1T_ch[bufq] = ps_m_ev[0]

            for t in range(4):
                for _ in gen_ln1_item(0, t):
                    pass

            for qc in range(NCH):
                buf = qc % 2
                if qc + 1 < NCH:
                    dep = [('V', ev_chunk_resid[qc - 1])] if qc >= 1 else []
                    sx = f'Lx{(qc + 1) % 2}'
                    ev_xload[qc + 1] = (sx, pl.emit(
                        'sync', lambda e, q=qc + 1: e.dma_start(xc[:, q % 2, :, :], x_t[q]),
                        deps=dep, inc=sx, k=16, tag=f'x{qc+1}'))
                if qc == 5:
                    plan_thr()

                ev_x1T_v = ev_x1T_ch[buf]
                ev_lat = 0
                for which, wsrc, dstT in [(0, wkd_b, klatT), (1, wvd_b, vlatT)]:
                    tl = 0
                    for i in range(8):
                        r = pl.emit('tensor',
                                    lambda e, i=i, w=wsrc, which=which, buf=buf: e.matmul(
                                        ps_a[0:64, which, :], w[:, i, :],
                                        x1T[:, buf, i, :],
                                        start=(i == 0), stop=(i == 7)),
                                    deps=[('V', ev_x1T_v), ps_slot[which]],
                                    inc=('T' if i == 7 else None),
                                    tag=f'lat{which}_{qc}_{i}')
                        tl = r if r is not None else tl
                    sE = pl.emit('scalar',
                                 lambda e, d=dstT, qc=qc, which=which: e.copy(
                                     d[:, qc * CH:(qc + 1) * CH], ps_a[0:64, which, :]),
                                 deps=[('T', tl)], inc='S', tag=f'late{which}_{qc}')
                    ps_slot[which] = ('S', sE)
                    ev_lat = sE
                for g in range(NKV):
                    tK = pl.emit('tensor',
                                 lambda e, g=g, qc=qc: e.matmul(
                                     ps_b[:, 0, :], wku_b[32 * g:32 * (g + 1), :],
                                     klatT[32 * g:32 * (g + 1), qc * CH:(qc + 1) * CH],
                                     start=True, stop=True),
                                 deps=[('S', ev_lat), ps_slot[2]], inc='T',
                                 tag=f'KT{qc}_{g}')
                    sE = pl.emit('scalar',
                                 lambda e, g=g, qc=qc: e.copy(
                                     KT[:, g, qc * CH:(qc + 1) * CH], ps_b[:, 0, :]),
                                 deps=[('T', tK)], inc='S', tag=f'KTe{qc}_{g}')
                    ps_slot[2] = ('S', sE)
                    tl = 0
                    for t in range(4):
                        kt = qc * 4 + t
                        tl = pl.emit('tensor',
                                     lambda e, g=g, t=t, kt=kt: e.matmul(
                                         ps_b[:, 1, t * 128:(t + 1) * 128],
                                         vlatT[32 * g:32 * (g + 1),
                                               kt * 128:(kt + 1) * 128],
                                         wvu_b[32 * g:32 * (g + 1), :],
                                         start=True, stop=True),
                                     deps=[('S', ev_lat), ps_slot[3]], inc='T',
                                     tag=f'V{qc}_{g}_{t}')
                    sE = pl.emit('scalar',
                                 lambda e, g=g, qc=qc: e.copy(
                                     Vsb[:, g, qc * 4:(qc + 1) * 4, :]
                                     .rearrange("p t d -> p (t d)"), ps_b[:, 1, :]),
                                 deps=[('T', tl)], inc='S', tag=f'Ve{qc}_{g}')
                    ps_slot[3] = ('S', sE)
                ev_KV = pl.c('S')
                dump(f'KT{qc}', KT[:, :, :], [('S', ev_KV)])
                dump(f'V{qc}', Vsb[:, :, :, :], [('S', ev_KV)])
                for h in range(H):
                    slot = h % 4
                    pst = [ps_a[:, 0, :], ps_a[:, 1, :], ps_b[:, 0, :], ps_b[:, 1, :]][slot]
                    tl = 0
                    for i in range(8):
                        r = pl.emit('tensor',
                                    lambda e, h=h, i=i, pst=pst, buf=buf: e.matmul(
                                        pst, wq_b[:, i, h * 128:(h + 1) * 128],
                                        x1T[:, buf, i, :], start=(i == 0), stop=(i == 7)),
                                    deps=[('V', ev_x1T_v), ps_slot[slot]],
                                    inc=('T' if i == 7 else None),
                                    tag=f'QT{qc}_{h}_{i}')
                        tl = r if r is not None else tl
                    sE = pl.emit('scalar',
                                 lambda e, h=h, pst=pst: e.copy(QT[:, h, :], pst),
                                 deps=[('T', tl)], inc='S', tag=f'QTe{qc}_{h}')
                    ps_slot[slot] = ('S', sE)
                ev_QTev[0] = pl.c('S')
                dump(f'QT{qc}', QT[:, :, :], [('S', ev_QTev[0])])

                side = []
                if qc + 1 < NCH:
                    for t in range(4):
                        side.append(gen_ln1_item(qc + 1, t))
                gamma_sched = {2: range(0, 28), 3: range(28, 64)}
                quant_sched = {5: range(0, 20), 6: range(20, 42),
                               7: range(42, 64)}
                for u in gamma_sched.get(qc, []):
                    side.append(gen_gamma_unit(u))
                for u in quant_sched.get(qc, []):
                    side.append(gen_quant_unit(u))
                nstage = sum(1 for _ in ()) or None

                nk = 4 * (qc + 1)
                tasks = [(h, kt) for h in range(H) for kt in range(nk)]
                nstages = (4 * (4 if qc + 1 < NCH else 0)
                           + len(gamma_sched.get(qc, []))
                           + 3 * len(quant_sched.get(qc, [])))
                drain_every = max(1, len(tasks) // (nstages + 1))
                qk_ev = {}
                sslots = [ps_a[:, 0, :], ps_a[:, 1, :], ps_b[:, 0, :], ps_b[:, 1, :]]

                def emit_qk(i):
                    h, kt = tasks[i]
                    g = h // 4
                    sl4 = i % 4
                    qk_ev[i] = pl.emit('tensor',
                                       lambda e, g=g, h=h, kt=kt, sl4=sl4: e.matmul(
                                           sslots[sl4],
                                           KT[:, g, kt * 128:(kt + 1) * 128],
                                           QT[:, h, :], start=True, stop=True),
                                       deps=[('S', ev_QTev[0]), ps_slot[sl4],
                                             ('S', ev_KV)],
                                       inc='T', tag=f'qk{qc}_{h}_{kt}')

                pend_ep = []

                def emit_ep_start(h, tl):
                    l1 = pl.emit('scalar',
                                 lambda e: e.activation(rs_f[:, :], ps_wo[0:1, 0, :],
                                                        AF.Ln),
                                 deps=[('T', tl), ('S', pl.c('S'))], inc='S',
                                 tag=f'rc{qc}_{h}')
                    ev_b0rd[0] = ('S', l1)
                    l2 = pl.emit('scalar',
                                 lambda e, h=h: e.activation(rs_b[0:1, h % 2, :],
                                                             rs_f[:, :], AF.Exp,
                                                             scale=cpool[0:1, 4:5]),
                                 deps=[('S', l1), ('T', ev_rsb_rd[h % 2])], inc='S',
                                 tag=f'rcb{qc}_{h}')
                    return (h, tl, l2)

                def emit_ep_finish(h, tl, l2, cur_i):
                    bslot = (cur_i - 1) % 4
                    tb = pl.emit('tensor',
                                 lambda e, bslot=bslot, h=h: e.matmul(
                                     sslots[bslot], ones_c[:, :],
                                     rs_b[:, h % 2, :], start=True, stop=True),
                                 deps=[('S', l2), ps_slot[bslot], ev_b1rd[0]],
                                 inc='T', tag=f'bc{qc}_{h}')
                    ev_rsb_rd[h % 2] = tb
                    sE = pl.emit('vector',
                                 lambda e, bslot=bslot: e.tensor_copy(
                                     rb_sb[:, :], sslots[bslot]),
                                 deps=[('T', tb), ('V', ev_psy_free[h % 2]),
                                       ('V', pl.c('V'))],
                                 inc='V', tag=f'rbe{qc}_{h}')
                    ps_slot[bslot] = ('V', sE)
                    ev_b1rd[0] = ('V', sE)
                    psy = ps_y[:, :] if h % 2 == 0 else ps_wo[:, 1, :]
                    v3 = pl.emit('vector',
                                 lambda e, h=h, psy=psy: e.tensor_tensor(
                                     yt[:, h, :], psy, rb_sb[:, :], ALU.mult),
                                 deps=[('V', sE), ('T', tl), ('T', ev_wo_lastT[0]),
                                       ('V', pl.c('V'))],
                                 inc='V', tag=f'yt{qc}_{h}')
                    ev_psy_free[h % 2] = v3

                emit_qk(0)
                emit_qk(1)
                emit_qk(2)
                for i, (h, kt) in enumerate(tasks):
                    g = h // 4
                    sl4 = i % 4
                    tqk = qk_ev[i]
                    sE = pl.emit('scalar',
                                 lambda e, sl4=sl4: e.activation(
                                     ptb[:, sl4, :], sslots[sl4], AF.Exp,
                                     bias=cpool[:, 3:4], scale=cpool[:, 2:3]),
                                 deps=[('T', tqk), ('T', pt_T[sl4]),
                                       ('G', ev_consts)],
                                 inc='S', tag=f'exp{qc}_{h}_{kt}')
                    ps_slot[sl4] = ('S', sE)
                    vm = 0
                    r = kt - 4 * qc
                    if r >= 0:
                        vm = pl.emit('vector',
                                     lambda e, sl4=sl4, r=r: e.tensor_tensor(
                                         ptb[:, sl4, :], ptb[:, sl4, :],
                                         masks[:, r, :], ALU.mult),
                                     deps=[('S', sE), ('V', ev_masks),
                                           ('V', pl.c('V'))],
                                     inc='V', tag=f'msk{qc}_{h}_{kt}')
                    if pend_ep and i >= pend_ep[0][0]:
                        emit_ep_finish(*pend_ep.pop(0)[1], i)
                    if i + 3 < len(tasks):
                        emit_qk(i + 3)
                    pdep = [('V', vm)] if vm else [('S', sE)]
                    pl.emit('tensor',
                            lambda e, sl4=sl4, stt=(kt == 0): e.matmul(
                                ps_wo[:, 0, :], ones_c[:, :],
                                ptb[:, sl4, :], start=stt, stop=True,
                                skip_group_check=True),
                            deps=pdep + [ev_b0rd[0], ev_woev[0]],
                            tag=f'ss{qc}_{h}_{kt}')
                    psy = ps_y[:, :] if h % 2 == 0 else ps_wo[:, 1, :]
                    tl = pl.emit('tensor',
                                 lambda e, g=g, kt=kt, sl4=sl4, stt=(kt == 0),
                                        psy=psy: e.matmul(
                                     psy, Vsb[:, g, kt, :],
                                     ptb[:, sl4, :], start=stt, stop=True,
                                     skip_group_check=True),
                                 deps=[('V', ev_psy_free[h % 2])] + pdep,
                                 inc='T', tag=f'pv{qc}_{h}_{kt}')
                    pt_T[sl4] = tl
                    if kt == nk - 1:
                        pend_ep.append((i + 5, emit_ep_start(h, tl)))
                    if side and (i + 1) % drain_every == 0:
                        try:
                            next(side[0])
                        except StopIteration:
                            side.pop(0)
                while side:
                    try:
                        next(side[0])
                    except StopIteration:
                        side.pop(0)
                while pend_ep:
                    emit_ep_finish(*pend_ep.pop(0)[1], len(tasks) + 4)
                ev_yt_all = pl.c('V')
                dump(f'yt{qc}', yt[:, :, :], [('V', ev_yt_all)])
                for t in range(4):
                    tile = qc * 4 + t
                    slot = t % 2
                    pool = [ps_wo, ps_a, ps_b][t % 3]
                    if t % 3 == 0:
                        pdeps = [ev_woev[0], ev_b0rd[0], ('V', ev_psy_free[1])]
                    elif t % 3 == 1:
                        pdeps = [ps_slot[0], ps_slot[1]]
                    else:
                        pdeps = [ps_slot[2], ps_slot[3]]
                    tl = 0
                    for h in range(H):
                        for j in range(2):
                            r = pl.emit('tensor',
                                        lambda e, h=h, j=j, t=t, pool=pool: e.matmul(
                                            pool[:, j, :],
                                            yt[:, h, t * 128:(t + 1) * 128],
                                            wo_b[:, h, j * 512:(j + 1) * 512],
                                            start=(h == 0), stop=(h == H - 1)),
                                        deps=[('V', ev_yt_all)] + pdeps,
                                        inc=('T' if h == H - 1 else None),
                                        tag=f'wo{qc}_{t}_{h}_{j}')
                            tl = r if r is not None else tl
                    ev_wo_lastT[0] = tl
                    sE = pl.emit('vector',
                                 lambda e, slot=slot, pool=pool: e.tensor_copy(
                                     asb[:, slot, :],
                                     pool[:, :, :].rearrange("p j c -> p (j c)")),
                                 deps=[('T', tl),
                                       ('Sxa' if slot == 0 else 'Sxb', ev_asb_free[slot]),
                                       ('V', ev_asb_ln3[slot]), ('V', pl.c('V'))],
                                 inc='V', tag=f'woe{qc}_{t}')
                    if t % 3 == 0:
                        ev_woev[0] = ('V', sE)
                        ev_psy_free[1] = sE
                    elif t % 3 == 1:
                        ps_slot[0] = ('V', sE)
                        ps_slot[1] = ('V', sE)
                    else:
                        ps_slot[2] = ('V', sE)
                        ps_slot[3] = ('V', sE)
                    pl.emit('vector',
                            lambda e, slot=slot: e.bn_stats(bn6b[:, 0:6],
                                                            asb[:, slot, 0:512]),
                            deps=[('V', sE), ('S', ev_r2b[0]), ('V', pl.c('V'))],
                            inc='V', tag=f'l2a{qc}_{t}')
                    b2 = pl.emit('vector',
                                 lambda e, slot=slot: e.bn_stats(bn6b[:, 6:12],
                                                                 asb[:, slot, 512:1024]),
                                 deps=[], inc='V', tag=f'l2b{qc}_{t}')
                    b3 = pl.emit('vector',
                                 lambda e: e.bn_aggr(mvln2[:, :], bn6b[:, :]),
                                 deps=[('V', b2)], inc='V', tag=f'l2c{qc}_{t}')
                    s1 = pl.emit('scalar',
                                 lambda e: e.activation(rtmp2[:, :], mvln2[:, 1:2], AF.Ln,
                                                        bias=cpool[:, 0:1]),
                                 deps=[('V', b3)], inc='S', tag=f'l2d{qc}_{t}')
                    s2 = pl.emit('scalar',
                                 lambda e: e.activation(rln2[:, :], rtmp2[:, :], AF.Exp,
                                                        scale=cpool[:, 1:2]),
                                 deps=[('S', s1)], inc='S', tag=f'l2e{qc}_{t}')
                    ev_r2b[0] = s2
                    v1 = pl.emit('vector',
                                 lambda e, slot=slot: e.tensor_scalar(
                                     asb[:, slot, :], asb[:, slot, :], mvln2[:, 0:1],
                                     rln2[:, 0:1], ALU.subtract, ALU.mult),
                                 deps=[('S', s2), ('V', pl.c('V'))], inc='V',
                                 tag=f'l2f{qc}_{t}')
                    v2 = pl.emit('vector',
                                 lambda e, slot=slot, t=t, buf=buf: e.tensor_tensor(
                                     asb[:, slot, :], asb[:, slot, :], xc[:, buf, t, :],
                                     ALU.add),
                                 deps=[('V', v1)], inc='V', tag=f'res{qc}_{t}')
                    if t == 3:
                        ev_chunk_resid[qc] = v2
                    pl.emit('vector',
                            lambda e, slot=slot: e.bn_stats(bn6b[:, 0:6],
                                                            asb[:, slot, 0:512]),
                            deps=[('V', v2)], inc='V', tag=f'l3a{qc}_{t}')
                    b2 = pl.emit('vector',
                                 lambda e, slot=slot: e.bn_stats(bn6b[:, 6:12],
                                                                 asb[:, slot, 512:1024]),
                                 deps=[], inc='V', tag=f'l3b{qc}_{t}')
                    b3 = pl.emit('vector',
                                 lambda e, tile=tile: e.bn_aggr(ln3mv[:, tile, :],
                                                                bn6b[:, :]),
                                 deps=[('V', b2)], inc='V', tag=f'l3c{qc}_{t}')
                    ev_asb_ln3[slot] = b3
                    ev_x1_store[tile] = pl.emit(
                        'gpsimd',
                        lambda e, slot=slot, tile=tile: e.dma_start(x1_t[tile],
                                                                    asb[:, slot, :]),
                        deps=[('V', v2)], inc=('Sxa' if slot == 0 else 'Sxb'),
                        k=16, tag=f'x1s{qc}_{t}')
                    ev_asb_free[slot] = ev_x1_store[tile]


            endA = {}
            endA['T'] = pl.emit('tensor', lambda e: e.nop(), inc='T', tag='fT')
            endA['V'] = pl.emit('vector', lambda e: e.nop(), inc='V', tag='fV')
            endA['S'] = pl.emit('scalar', lambda e: e.nop(), inc='S', tag='fS')
            endA['G'] = pl.emit('gpsimd', lambda e: e.nop(), inc='G', tag='fG')
            endA['Sxa'] = pl.c('Sxa')
            endA['Sxb'] = pl.c('Sxb')
            endA['Sf8'] = pl.c('Sf8')
            fence = [('T', endA['T']), ('V', endA['V']), ('S', endA['S']),
                     ('G', endA['G'])]
            pl.flush(nc, semh)

        with ExitStack() as stB:
            eb = stB.enter_context
            wfc8 = eb(nc.sbuf_tensor("wfc8", [128, NT, 8, 128], FP8))
            wp8 = eb(nc.sbuf_tensor("wp8", [128, 8, NT, 128], FP8))
            w_rep = eb(nc.sbuf_tensor("w_rep", [128, HFF], BF16))
            x1t = eb(nc.sbuf_tensor("x1t", [128, 8, D], F32))
            h_g = eb(nc.sbuf_tensor("h_g", [128, 2, HFF], F32))
            zb = eb(nc.sbuf_tensor("zb", [128, 2, HFF], BF16))
            x2 = eb(nc.sbuf_tensor("x2", [128, 2, D], BF16))
            x2T = eb(nc.sbuf_tensor("x2T", [128, 2, 8, 128], BF16))
            gsum = eb(nc.sbuf_tensor("gsum", [128, 2, 8], F32))
            qsum = eb(nc.sbuf_tensor("qsum", [128, 2, 8], F32))
            sqd = eb(nc.sbuf_tensor("sqd", [128, 512], BF16))
            stt1 = eb(nc.sbuf_tensor("stt1", [128, 4], F32))
            zT = eb(nc.sbuf_tensor("zT", [128, NT, 128], BF16))
            o2sb = eb(nc.sbuf_tensor("o2sb", [128, 2, D], F32))
            bwq_rep = eb(nc.sbuf_tensor("bwq_rep", [128, D], F32))
            outb = eb(nc.sbuf_tensor("outb", [128, 2, D], F32))
            r3all = eb(nc.sbuf_tensor("r3all", [128, NT], F32))
            r3tmp = eb(nc.sbuf_tensor("r3tmp", [128, NT], F32))
            hmv = eb(nc.sbuf_tensor("hmv", [128, 4, 2], F32))
            rhh = eb(nc.sbuf_tensor("rhh", [128, 2], F32))
            rht = eb(nc.sbuf_tensor("rht", [128, 2], F32))
            o4mv = eb(nc.sbuf_tensor("o4mv", [128, 2, 2], F32))
            r44 = eb(nc.sbuf_tensor("r44", [128, 2], F32))
            r4t = eb(nc.sbuf_tensor("r4t", [128, 2], F32))
            bnB = eb(nc.sbuf_tensor("bnB", [128, 12], F32))

            ps_h = eb(nc.psum_tensor("ps_h", [128, 2, 512], F32))
            ps_ztp = eb(nc.psum_tensor("ps_ztp", [128, 2, 16, 128], BF16))
            ps_o2 = eb(nc.psum_tensor("ps_o2", [128, 2, 512], F32))

            pl.emit('sync', lambda e: e.dma_start(wfc8[:, :, :, :], f8a_d),
                    deps=[('Sf8', endA['Sf8']), ('Sxa', endA['Sxa']),
                          ('Sxb', endA['Sxb'])] + fence,
                    inc='L8', k=16, tag='ldf8a')
            ldB = pl.emit('sync', lambda e: e.dma_start(wp8[:, :, :, :], f8b_d),
                          deps=[], inc='L8', k=16, tag='ldf8b')
            ev_w8 = ('L8', ldB)
            ldA = ldB

            ev_wr = 0
            for c in range(8):
                tb = pl.emit('tensor',
                             lambda e, c=c: e.matmul(ps_o2[:, 0, :], ones_r[:, :],
                                                     lnw_row[0:1, c * 512:(c + 1) * 512],
                                                     start=True, stop=True),
                             deps=fence + [('V', ev_wr), ('Lw', ev_wload)],
                             inc='T', tag=f'wr{c}')
                ev_wr = pl.emit('vector',
                                lambda e, c=c: e.tensor_copy(
                                    w_rep[:, c * 512:(c + 1) * 512], ps_o2[:, 0, :]),
                                deps=[('T', tb)], inc='V', tag=f'wre{c}')

            s1 = pl.emit('scalar',
                         lambda e: e.activation(r3tmp[:, :], ln3mv[:, :, 1], AF.Ln,
                                                bias=cpool[:, 0:1]),
                         deps=fence, inc='S', tag='r3a')
            ev_r3 = pl.emit('scalar',
                            lambda e: e.activation(r3all[:, :], r3tmp[:, :], AF.Exp,
                                                   scale=cpool[:, 1:2]),
                            deps=[('S', s1)], inc='S', tag='r3b')

            tl = 0
            for dc in range(2):
                for s in range(NT):
                    r = pl.emit('tensor',
                                lambda e, dc=dc, s=s: e.matmul(
                                    ps_h[0:1, dc, :], lnb_sb[:, s:s + 1],
                                    wp8[:, 4 * dc:4 * dc + 4, s, :],
                                    start=(s == 0), stop=(s == NT - 1)),
                                deps=[ev_w8, ('Lw', ev_wload)],
                                inc=('T' if s == NT - 1 else None),
                                tag=f'bwq{dc}_{s}')
                    tl = r if r is not None else tl
            ev_bwq = pl.emit('scalar',
                             lambda e: e.copy(bwq_sb[:, :],
                                              ps_h[0:1, :, :].rearrange("p j c -> p (j c)")),
                             deps=[('T', tl)], inc='S', tag='bwqe')
            ev_brep = 0
            for dc in range(2):
                tb = pl.emit('tensor',
                             lambda e, dc=dc: e.matmul(ps_o2[:, dc, :], ones_r[:, :],
                                                       bwq_sb[0:1, dc * 512:(dc + 1) * 512],
                                                       start=True, stop=True),
                             deps=[('S', ev_bwq), ('V', ev_wr)], inc='T',
                             tag=f'brep{dc}')
                ev_brep = pl.emit('vector',
                                  lambda e, dc=dc: e.tensor_copy(
                                      bwq_rep[:, dc * 512:(dc + 1) * 512],
                                      ps_o2[:, dc, :]),
                                  deps=[('T', tb)], inc='V', tag=f'brepe{dc}')
            ev_o2_rd0 = ev_brep

            dump('wfc8', wfc8[:, :, :, :], [('L8', ldA)], cast=F32)
            dump('wp8', wp8[:, :, :, :], [('L8', ldB)], cast=F32)
            dump('bwq', bwq_sb[:, :], [('S', ev_bwq)], cast=F32)
            dump('wrep', w_rep[:, :], [('V', ev_wr)], cast=F32)
            ev_x1load = [None] * NT
            ev_x1slot_free = [0] * 8
            ev_hslot_free = [0, 0]
            ev_x2T_ev = [0]
            ev_fc_last = [0]
            ev_zT_free = [0]
            ev_o2sb_free = [0, 0]
            ev_o2e = [ev_wr, ev_wr]
            ev_o2_rd = [None]
            ev_outb_free = [0, 0]
            ev_ztp_ev = [0]
            ev_gelu = [0, 0]
            ev_hmv = [0, 0, 0, 0]
            ev_o4mv = [0, 0]
            ev_rh = [0]
            ev_r4 = [0]
            pend = []

            for t in range(4):
                sx = f'Lb{t % 8}'
                ev_x1load[t] = (sx, pl.emit('sync',
                                       lambda e, t=t: e.dma_start(x1t[:, t, :], x1_t[t]),
                                       deps=[('Sxa', endA['Sxa']),
                                             ('Sxb', endA['Sxb'])] + fence,
                                       inc=sx, k=16, tag=f'bx{t}'))

            def ln4_apply(tile):
                par = tile % 2
                slot = tile % 8
                v1 = pl.emit('vector',
                             lambda e, par=par: e.tensor_scalar(
                                 o2sb[:, par, :], o2sb[:, par, :], o4mv[:, par, 0:1],
                                 r44[:, par:par + 1], ALU.subtract, ALU.mult),
                             deps=[('S', ev_r4[0]), ('V', ev_o2e[par]),
                                   ('V', pl.c('V'))],
                             inc='V', tag=f'l4f{tile}')
                v2 = pl.emit('vector',
                             lambda e, par=par, slot=slot: e.tensor_tensor(
                                 outb[:, par, :], o2sb[:, par, :], x1t[:, slot, :],
                                 ALU.add),
                             deps=[('V', v1), (f'So{par}', ev_outb_free[par])],
                             inc='V', tag=f'l4r{tile}')
                if tile == 0:
                    dump('ap_o4mv', o4mv[:, :, :], [('S', ev_r4[0])])
                    dump('ap_r44', r44[:, :], [('S', ev_r4[0])])
                    dump('ap_o2sb', o2sb[:, 0, :], [('V', v1)])
                    dump('ap_x1t', x1t[:, 0, :], [('V', v2)])
                    dump('ap_outb', outb[:, 0, :], [('V', v2)])
                ev_x1slot_free[slot] = v2
                ev_o2sb_free[par] = v2
                ev_outb_free[par] = pl.emit(
                    'gpsimd',
                    lambda e, par=par, tile=tile: e.dma_start(out_t[tile],
                                                              outb[:, par, :]),
                    deps=[('V', v2)], inc=f'So{par}', k=16, tag=f'outs{tile}')

            ev_ztp = [0, 0]
            ev_o2_rd[0] = ev_o2_rd0
            ev_r44_rd = [0]

            ev_x2st = [('V', 0), ('V', 0)]

            def plan_x2stage(tile):
                xb = tile % 2
                v1 = pl.emit('vector',
                             lambda e, tile=tile, slot=tile % 8, xb=xb: e.tensor_scalar(
                                 x2[:, xb, :], x1t[:, slot, :], ln3mv[:, tile, 0:1],
                                 r3all[:, tile:tile + 1], ALU.subtract, ALU.mult),
                             deps=[ev_x1load[tile], ('S', ev_r3),
                                   ('T', ev_fc_last[0]), ('V', pl.c('V'))],
                             inc='V', tag=f'x2_{tile}')
                tl = 0
                for i in range(8):
                    tl = pl.emit('tensor',
                                 lambda e, i=i, xb=xb: e.transpose(
                                     ps_ztp[:, 0, i, :],
                                     x2[:, xb, i * 128:(i + 1) * 128], ident[:, :]),
                                 deps=[('V', v1), ('V', ev_ztp[0])],
                                 inc=('T' if i == 7 else None), tag=f'x2T{tile}_{i}')
                vE = pl.emit('vector',
                             lambda e, xb=xb: e.tensor_copy(x2T[:, xb, :, :],
                                                            ps_ztp[:, 0, 0:8, :]),
                             deps=[('T', tl)], inc='V', tag=f'x2Te{tile}')
                ev_ztp[0] = vE
                ev_x2st[xb] = ('V', vE)

            def plan_fc(tile):
                par = tile % 2
                xb = tile % 2
                for hc in range(8):
                    pb = hc % 2
                    tl = 0
                    for i in range(8):
                        r = pl.emit('tensor',
                                    lambda e, hc=hc, i=i, pb=pb, xb=xb: e.matmul(
                                        ps_h[:, pb, :], x2T[:, xb, i, :],
                                        wfc8[:, 4 * hc:4 * hc + 4, i, :],
                                        start=(i == 0), stop=(i == 7)),
                                    deps=[ev_x2st[xb], ev_w8,
                                          ('S', ev_gelu[pb]), ('S', ev_bwq)],
                                    inc=('T' if i == 7 else None),
                                    tag=f'fc{tile}_{hc}_{i}')
                        tl = r if r is not None else tl
                    ev_gelu[pb] = pl.emit(
                        'scalar',
                        lambda e, hc=hc, pb=pb, par=par: e.activation(
                            h_g[:, par, hc * 512:(hc + 1) * 512], ps_h[:, pb, :],
                            AF.Gelu, accum_out=gsum[:, par, hc:hc + 1]),
                        deps=[('T', tl), ('V', ev_hslot_free[par])],
                        inc='S', tag=f'gel{tile}_{hc}')
                    pl.emit('scalar',
                            lambda e, hc=hc, par=par: e.activation(
                                sqd[:, :], h_g[:, par, hc * 512:(hc + 1) * 512],
                                AF.Square, accum_out=qsum[:, par, hc:hc + 1]),
                            deps=[('S', ev_gelu[pb])], inc='S', tag=f'sq{tile}_{hc}')
                ev_fc_last[0] = tl
                ev_sq = pl.c('S')
                v1 = pl.emit('vector',
                             lambda e, par=par: e.tensor_reduce(
                                 stt1[:, 0:1], gsum[:, par, :], AX.X, ALU.add),
                             deps=[('S', ev_sq), ('V', pl.c('V'))], inc='V',
                             tag=f'st1{tile}')
                v2 = pl.emit('vector',
                             lambda e, par=par: e.tensor_reduce(
                                 stt1[:, 1:2], qsum[:, par, :], AX.X, ALU.add),
                             deps=[], inc='V', tag=f'st2{tile}')
                h4 = tile % 4
                v3 = pl.emit('vector',
                             lambda e, h4=h4: e.tensor_scalar(
                                 hmv[:, h4, 0:1], stt1[:, 0:1], 1.0 / HFF, None,
                                 ALU.mult),
                             deps=[('V', v2), ('S', ev_rh[0])], inc='V',
                             tag=f'st3{tile}')
                v4 = pl.emit('vector',
                             lambda e, h4=h4: e.tensor_tensor(
                                 stt1[:, 2:3], hmv[:, h4, 0:1],
                                 hmv[:, h4, 0:1], ALU.mult),
                             deps=[('V', v3)], inc='V', tag=f'st4{tile}')
                ev_hmv[h4] = pl.emit('vector',
                                     lambda e, h4=h4: e.scalar_tensor_tensor(
                                         hmv[:, h4, 1:2], stt1[:, 1:2], 1.0 / HFF,
                                         stt1[:, 2:3], ALU.mult, ALU.subtract),
                                     deps=[('V', v4)], inc='V', tag=f'st5{tile}')
                if tile <= 1:
                    dump(f'hmv{tile}', hmv[:, :, :], [('V', ev_hmv[h4])])
                    dump(f'hg{tile}', h_g[:, tile % 2, :], [('V', ev_hmv[h4])])

            def plan_rstd_h(g):
                c0 = (2 * g) % 4
                s1 = pl.emit('scalar',
                             lambda e, c0=c0: e.activation(rht[:, :],
                                                           hmv[:, c0:c0 + 2, 1],
                                                           AF.Ln, bias=cpool[:, 0:1]),
                             deps=[('V', ev_hmv[c0]), ('V', ev_hmv[c0 + 1]),
                                   ('V', pl.c('V'))],
                             inc='S', tag=f'rha{g}')
                ev_rh[0] = pl.emit('scalar',
                                   lambda e: e.activation(rhh[:, :], rht[:, :], AF.Exp,
                                                          scale=cpool[:, 1:2]),
                                   deps=[('S', s1)], inc='S', tag=f'rhb{g}')

            def plan_r4(g):
                s3 = pl.emit('scalar',
                             lambda e: e.activation(r4t[:, :], o4mv[:, :, 1], AF.Ln,
                                                    bias=cpool[:, 0:1]),
                             deps=[('V', ev_o4mv[0]), ('V', ev_o4mv[1]),
                                   ('V', ev_r44_rd[0])],
                             inc='S', tag=f'r4a{g}')
                ev_r4[0] = pl.emit('scalar',
                                   lambda e: e.activation(r44[:, :], r4t[:, :], AF.Exp,
                                                          scale=cpool[:, 1:2]),
                                   deps=[('S', s3)], inc='S', tag=f'r4b{g}')

            def plan_applies():
                for pt_ in pend:
                    ln4_apply(pt_)
                if pend:
                    ev_r44_rd[0] = pl.c('V')
                pend.clear()

            ev_zbfree = [('T', 0), ('T', 0)]

            def plan_z(tt):
                pz = tt % 2
                v1 = pl.emit('vector',
                             lambda e, pz=pz, h4=tt % 4: e.tensor_scalar(
                                 h_g[:, pz, :], h_g[:, pz, :], hmv[:, h4, 0:1],
                                 None, ALU.subtract),
                             deps=[('V', ev_hmv[tt % 4]), ('V', pl.c('V'))],
                             inc='V', tag=f'z1_{tt}')
                v2 = pl.emit('vector',
                             lambda e, pz=pz: e.tensor_tensor(
                                 zb[:, pz, :], h_g[:, pz, :], w_rep[:, :], ALU.mult),
                             deps=[('V', v1), ev_zbfree[pz], ('V', ev_wr)],
                             inc='V', tag=f'z2_{tt}')
                ev_hslot_free[pz] = v2
                return v2

            def plan_ztproj(tt, v2):
                pz = tt % 2
                for w in range(2):
                    tl = 0
                    for i in range(16):
                        s_ = w * 16 + i
                        tl = pl.emit('tensor',
                                     lambda e, s_=s_, i=i, w=w, pz=pz: e.transpose(
                                         ps_ztp[:, w, i, :],
                                         zb[:, pz, s_ * 128:(s_ + 1) * 128],
                                         ident[:, :]),
                                     deps=[('V', v2), ('V', ev_ztp[w])],
                                     inc=('T' if i == 15 else None),
                                     tag=f'zT{tt}_{s_}')
                    ev_ztp[w] = pl.emit(
                        'vector',
                        lambda e, w=w: e.tensor_copy(
                            zT[:, w * 16:(w + 1) * 16, :], ps_ztp[:, w, :, :]),
                        deps=[('T', tl)], inc='V', tag=f'zTe{tt}_{w}')
                ev_zbfree[pz] = ('T', tl)
                tl = 0
                for dc in range(2):
                    for s_ in range(NT):
                        r = pl.emit('tensor',
                                    lambda e, dc=dc, s_=s_: e.matmul(
                                        ps_o2[:, dc, :], zT[:, s_, :],
                                        wp8[:, 4 * dc:4 * dc + 4, s_, :],
                                        start=(s_ == 0), stop=(s_ == NT - 1)),
                                    deps=[('V', ev_ztp[0]), ('V', ev_ztp[1]),
                                          ('V', ev_o2_rd[0])],
                                    inc=('T' if s_ == NT - 1 else None),
                                    tag=f'pj{tt}_{dc}_{s_}')
                        tl = r if r is not None else tl
                ev_zT_free[0] = tl
                v4 = pl.emit('vector',
                             lambda e, pz=pz: e.scalar_tensor_tensor(
                                 o2sb[:, pz, :],
                                 ps_o2[:, :, :].rearrange("p j c -> p (j c)"),
                                 rhh[:, pz:pz + 1], bwq_rep[:, :],
                                 ALU.mult, ALU.add),
                             deps=[('T', tl), ('V', ev_o2sb_free[pz]),
                                   ('S', ev_rh[0]), ('V', ev_brep)],
                             inc='V', tag=f'o2e{tt}')
                ev_o2e[pz] = v4
                ev_o2_rd[0] = v4
                if tt == 0:
                    dump('o2sb0', o2sb[:, 0, :], [('V', v4)])
                    dump('rhh0', rhh[:, :], [('V', v4)])
                b1 = pl.emit('vector',
                             lambda e, pz=pz: e.bn_stats(bnB[:, 0:6], o2sb[:, pz, 0:512]),
                             deps=[('V', v4)], inc='V', tag=f'l4a{tt}')
                b2 = pl.emit('vector',
                             lambda e, pz=pz: e.bn_stats(bnB[:, 6:12],
                                                         o2sb[:, pz, 512:1024]),
                             deps=[], inc='V', tag=f'l4b{tt}')
                b3 = pl.emit('vector',
                             lambda e, pz=pz: e.bn_aggr(o4mv[:, pz, :], bnB[:, :]),
                             deps=[('V', b2)], inc='V', tag=f'l4c{tt}')
                ev_o4mv[pz] = b3
                pend.append(tt)

            plan_x2stage(0)
            for tile in range(NT):
                if tile + 1 < NT:
                    plan_x2stage(tile + 1)
                zpair = None
                if tile >= 2 and tile % 2 == 0:
                    plan_applies()
                    zpair = (plan_z(tile - 2), plan_z(tile - 1))
                plan_fc(tile)
                if zpair is not None:
                    plan_ztproj(tile - 2, zpair[0])
                    plan_ztproj(tile - 1, zpair[1])
                if tile % 2 == 1:
                    plan_rstd_h((tile - 1) // 2)
                    if tile >= 3:
                        plan_r4((tile - 3) // 2)
                if tile + 4 < NT:
                    nt_ = tile + 4
                    sx = f'Lb{nt_ % 8}'
                    ev_x1load[nt_] = (sx, pl.emit(
                        'sync',
                        lambda e, nt_=nt_: e.dma_start(x1t[:, nt_ % 8, :], x1_t[nt_]),
                        deps=[('Sxa', endA['Sxa']), ('Sxb', endA['Sxb']),
                              ('V', ev_x1slot_free[nt_ % 8])],
                        inc=sx, k=16, tag=f'bx{nt_}'))

            plan_applies()
            za = plan_z(NT - 2)
            zbv = plan_z(NT - 1)
            plan_ztproj(NT - 2, za)
            plan_ztproj(NT - 1, zbv)
            plan_r4(NT // 2 - 1)
            plan_applies()

            pl.emit('gpsimd', lambda e: None,
                    deps=[('So0', pl.c('So0')), ('So1', pl.c('So1'))], tag='final')
            pl.flush(nc, semh)

    return nc, dump_info


def kernel(L0_attn_x, L0_attn_wq, L0_attn_wk_down, L0_attn_wv_down,
           L0_attn_wk_up, L0_attn_wv_up, L0_attn_wo,
           L0_ffn_W_fc, L0_ffn_ln_w, L0_ffn_ln_b, L0_ffn_W_proj, _dumps=()):
    f = lambda a: np.ascontiguousarray(np.asarray(a, dtype=np.float32))
    x = f(L0_attn_x)
    sel = np.zeros((2, 256), dtype=np.float32)
    sel[0, 0:128] = 1.0
    sel[1, 128:256] = 1.0
    ws = {
        "wq": f(L0_attn_wq), "wkd": f(L0_attn_wk_down), "wvd": f(L0_attn_wv_down),
        "wku": f(L0_attn_wk_up), "wvu": f(L0_attn_wv_up), "wo": f(L0_attn_wo),
        "wfc": f(L0_ffn_W_fc), "lnw": f(L0_ffn_ln_w), "lnb": f(L0_ffn_ln_b),
        "wpj": f(L0_ffn_W_proj),
        "ident2": np.eye(128, dtype=np.float32),
        "selc": sel,
    }
    nc, _ = build_program(dumps=_dumps)
    in_maps = [dict(ws, x=x[i]) for i in range(8)]
    res = run_bass_kernel_spmd(nc, in_maps, core_ids=list(range(8)))
    out = np.stack([res.results[i]["out"] for i in range(8)], axis=0)
    if _dumps:
        kernel._last_results = res
    return out.reshape(8, T, D)
